# version 31
# baseline (speedup 1.0000x reference)
import numpy as np
import ml_dtypes
import concourse.bass as bass
import concourse.mybir as mybir

F32 = mybir.dt.float32
BF16 = mybir.dt.bfloat16
AF = mybir.ActivationFunctionType
ALU = mybir.AluOpType
AX = mybir.AxisListType

D = 1024
NK = 8
H = 512
NHC = 4
E = 8
HEADS = 16
NPAIR = 8

ENGS = ("PE", "ACT", "DVE", "POOL", "SP")


class Plan:
    NDMA = 4

    def __init__(self):
        self.items = {e: [] for e in ENGS}
        self.cnt = {}
        self.waited = {}
        self.dmai = {e: 0 for e in ENGS}

    def op(self, eng, fn, deps=(), sig=True, dma=False):
        if dma:
            sem = "d" + eng + str(self.dmai[eng] % self.NDMA)
            self.dmai[eng] += 1
            step = 16
            deps = list(deps) + [(sem, self.cnt.get(sem, 0))]
        else:
            sem = "c" + eng
            step = 1
        waits = []
        for d in deps:
            if d is None:
                continue
            s, c = d
            if c <= 0:
                continue
            k = (eng, s)
            if self.waited.get(k, 0) < c:
                waits.append((s, c))
                self.waited[k] = c
        inc = None
        tok = None
        if sig:
            self.cnt[sem] = self.cnt.get(sem, 0) + step
            inc = (sem, step)
            tok = (sem, self.cnt[sem])
        self.items[eng].append((fn, waits, inc))
        return tok

    def qbar(self, eng):
        out = []
        for i in range(self.NDMA):
            sem = f"d{eng}{i}"
            c = self.cnt.get(sem, 0)
            if c > 0:
                out.append((sem, c))
        return out

    def dma(self, eng, dst, src, deps=(), **kw):
        return self.op(eng, lambda e, dst=dst, src=src, kw=kw: e.dma_start(out=dst, in_=src, **kw),
                       deps=deps, dma=True)

    def emit(self, eng_name, eng, sems):
        for fn, waits, inc in self.items[eng_name]:
            for s, c in waits:
                eng.wait_ge(sems[s], c)
            ins = fn(eng)
            if inc is not None:
                ins.then_inc(sems[inc[0]], inc[1])


class Res:
    __slots__ = ("w", "rs")
    def __init__(self):
        self.w = None
        self.rs = []
    def rdep(self):
        return [self.w] if self.w else []
    def wdep(self):
        d = list(self.rs)
        if self.w:
            d.append(self.w)
        return d
    def note_r(self, tok):
        if tok:
            self.rs.append(tok)
    def note_w(self, tok):
        if tok:
            self.w = tok
            self.rs = []


def build_nc(T=2048, dbg=False):
    assert T % 512 == 0
    NT = T // 128
    SW = 512
    NS = T // SW
    NST = SW // 128
    NC = NT

    nc = bass.Bass("TRN2", target_bir_lowering=False, debug=False)

    x_d = nc.dram_tensor("x", [T, D], F32, kind="ExternalInput").ap()
    out_d = nc.dram_tensor("out", [T, D], F32, kind="ExternalOutput").ap()
    watt_d = [nc.dram_tensor(n, [D, D], F32, kind="ExternalInput").ap()
              for n in ("W_Q", "W_K", "W_V", "W_K_prime", "W_O")]
    gw_d = nc.dram_tensor("gate_W", [D, E], F32, kind="ExternalInput").ap()
    w1_d = nc.dram_tensor("W1", [E, D, H], F32, kind="ExternalInput").ap()
    b1_d = nc.dram_tensor("b1", [E, H], F32, kind="ExternalInput").ap()
    w2_d = nc.dram_tensor("W2", [E, H, D], F32, kind="ExternalInput").ap()
    b2_d = nc.dram_tensor("b2", [E, D], F32, kind="ExternalInput").ap()
    ws1_d = nc.dram_tensor("Ws1", [D, H], F32, kind="ExternalInput").ap()
    bs1_d = nc.dram_tensor("bs1", [H], F32, kind="ExternalInput").ap()
    ws2_d = nc.dram_tensor("Ws2", [H, D], F32, kind="ExternalInput").ap()
    bs2_d = nc.dram_tensor("bs2", [D], F32, kind="ExternalInput").ap()
    alpha_d = nc.dram_tensor("alpha_bc", [128, 1], F32, kind="ExternalInput").ap()
    rsc_d = nc.dram_tensor("rs_bc", [128, 1], F32, kind="ExternalInput").ap()
    identf_d = nc.dram_tensor("identf", [128, 128], F32, kind="ExternalInput").ap()
    identb_d = nc.dram_tensor("identb", [128, 128], BF16, kind="ExternalInput").ap()
    onesb_d = nc.dram_tensor("ones_bf", [1, 128], BF16, kind="ExternalInput").ap()
    xpark_d = nc.dram_tensor("xpark", [T, D], F32, kind="Internal").ap()
    w1bf_d = nc.dram_tensor("w1bf", [2, 128, NK * H], BF16, kind="Internal").ap()
    w2bf_d = nc.dram_tensor("w2bf", [2, 128, NHC * D], BF16, kind="Internal").ap()
    wtbf_d = nc.dram_tensor("wtbf", [4, NK, 128, D], BF16, kind="Internal").ap()
    wotbf_d = nc.dram_tensor("wotbf", [NK, 128, D], BF16, kind="Internal").ap()

    A_f = nc.alloc_sbuf_tensor("arenaA", [128, NT * D], F32)
    A_b = A_f.bitcast(BF16)
    B_b = nc.alloc_sbuf_tensor("arenaB", [128, NK * T], BF16)
    C_cols = max(NK * T, HEADS * NC * 66)
    C_b = nc.alloc_sbuf_tensor("arenaC", [128, C_cols], BF16)
    D_b = nc.alloc_sbuf_tensor("arenaD", [128, max(8 * T, 16384)], BF16)
    E_f = nc.alloc_sbuf_tensor("arenaE", [128, 4608], F32)
    E_b = E_f.bitcast(BF16)

    gelu_s = [nc.alloc_sbuf_tensor(f"gelu{i}", [128, SW], F32) for i in range(2)]
    xn_s = [nc.alloc_sbuf_tensor(f"xnb{i}", [128, D], BF16) for i in range(2)]
    stat_s = [nc.alloc_sbuf_tensor(f"stat{i}", [128, 16], F32) for i in range(3)]

    identf_s = nc.alloc_sbuf_tensor("identf_s", [128, 128], F32)
    identb_s = nc.alloc_sbuf_tensor("identb_s", [128, 128], BF16)
    onesb_s = nc.alloc_sbuf_tensor("onesb_s", [1, 128], BF16)
    alpha_s = nc.alloc_sbuf_tensor("alpha_s", [128, 1], F32)
    oma_s = nc.alloc_sbuf_tensor("oma_s", [128, 1], F32)
    rsc_s = nc.alloc_sbuf_tensor("rsc_s", [128, 1], F32)
    eps_s = nc.alloc_sbuf_tensor("eps_s", [128, 1], F32)
    one_s = nc.alloc_sbuf_tensor("one_s", [128, 1], F32)
    b1_s = nc.alloc_sbuf_tensor("b1_s", [128, 40], F32)
    b2row_s = nc.alloc_sbuf_tensor("b2row_s", [E, D], BF16)
    bs2row_s = nc.alloc_sbuf_tensor("bs2row_s", [1, D], BF16)
    gateW_s = nc.alloc_sbuf_tensor("gateW_s", [128, NK * E], BF16)
    gfm_s = nc.alloc_sbuf_tensor("gfm_s", [E, SW], F32)
    gfin_s = nc.alloc_sbuf_tensor("gfin_s", [E, T], BF16)
    g_s = nc.alloc_sbuf_tensor("g_s", [128, NT * E], F32)
    gsc_s = nc.alloc_sbuf_tensor("gsc_s", [128, 32], F32)
    s2row_s = nc.alloc_sbuf_tensor("s2row_s", [1, SW], F32)
    rdrow_s = nc.alloc_sbuf_tensor("rdrow_s", [1, SW], F32)
    rsinv_s = nc.alloc_sbuf_tensor("rsinv_s", [1, 2 * SW], F32)
    mk_s = nc.alloc_sbuf_tensor("mk_s", [1, 256], BF16)

    PS = [nc.alloc_psum_tensor(f"ps{i}", [128, 1024], F32) for i in range(4)]
    PSH = []
    for i in range(4):
        PSH.append(PS[i][:, 0:512])
        PSH.append(PS[i][:, 512:1024])
    PS_bf = [p.bitcast(BF16) for p in PS]

    P = Plan()

    r_bank = [Res() for _ in range(8)]
    r_x = [Res() for _ in range(NT * 2)]
    r_xnT = Res()
    r_xn = [Res(), Res()]
    r_stat = [Res() for _ in range(3)]
    r_gelu = [Res() for _ in range(2)]
    r_w1 = [Res() for _ in range(2)]
    r_w2 = [Res() for _ in range(2)]
    r_hg = [Res() for _ in range(2)]
    r_stage = [Res() for _ in range(6)]
    r_wrow = [Res() for _ in range(3)]
    r_wts = [Res() for _ in range(2)]
    r_gfm = Res()
    r_gfin = Res()
    r_grow = [Res() for _ in range(2)]
    r_g = Res()
    r_gsc = [Res() for _ in range(2)]
    r_s2row = Res()
    r_rdrow = Res()
    r_gmk = Res()
    r_A = [Res() for _ in range(4)]
    r_oun = [Res() for _ in range(2 * NS)]
    r_rsinv = [Res(), Res()]
    r_kD = Res()
    r_vfm = [Res() for _ in range(2)]
    r_vones = Res()
    r_vaug = Res()
    r_wt = [Res() for _ in range(4)]
    r_park = [Res() for _ in range(NT)]
    r_t12 = Res()
    r_scr = Res()

    x_t = lambda tt: A_f[:, tt * D:(tt + 1) * D]
    x_h = lambda tt, dc: A_f[:, tt * D + dc * 512: tt * D + (dc + 1) * 512]
    q_ch = lambda m: A_b[:, m * T:(m + 1) * T]
    kp_ch = lambda m: A_b[:, NK * T + m * T: NK * T + (m + 1) * T]
    xnT_ch = lambda k: B_b[:, k * T:(k + 1) * T]
    omix_ch = lambda m: B_b[:, m * T:(m + 1) * T]
    k_ch = lambda m: D_b[:, m * T:(m + 1) * T]
    wo_ch = lambda dc: D_b[:, 8192 + dc * D: 8192 + (dc + 1) * D]
    w1v = lambda buf: D_b[:, buf * 4096:(buf + 1) * 4096]
    w2v = lambda buf: D_b[:, 8192 + buf * 4096: 8192 + (buf + 1) * 4096]
    hg_v = lambda buf, hc: C_b[:, (buf * 4 + hc) * T:(buf * 4 + hc + 1) * T]
    vaug = lambda h, c: C_b[:, (h * NC + c) * 66:(h * NC + c) * 66 + 65]
    stg = lambda s: (E_f[:, s * 512:(s + 1) * 512] if s < 4 else gelu_s[s - 4][:])
    NSTG = 6
    _stg_i = [0]
    def stg_slot():
        s = _stg_i[0] % NSTG
        _stg_i[0] += 1
        return s
    SQ = ("SP", "ACT", "POOL")
    _q_i = [0]
    def nxq():
        q = SQ[_q_i[0] % 3]
        _q_i[0] += 1
        return q
    wrow_v = lambda buf: (E_b[:, 4096 + buf * 1024: 4096 + (buf + 1) * 1024] if buf < 2
                          else E_b[:, 8192:9216])
    wts_v = lambda buf: E_b[:, 6144 + buf * 1024: 6144 + (buf + 1) * 1024]
    vfm_v = lambda buf: E_b[:, buf * 2048:(buf + 1) * 2048]
    gelu_b = [g.bitcast(BF16) for g in gelu_s]
    A_v = lambda buf: (E_b[:, buf * 1024:(buf + 1) * 1024] if buf < 2 else gelu_b[buf - 2][:, 0:1024])
    oun_v = lambda br, s: E_b[:, 2048 + (br * NS + s) * SW: 2048 + (br * NS + s + 1) * SW]
    gmk_v = E_f[0:E, 4096:4096 + SW]
    t2_s = E_f[:, 3584:4096]
    scr_row = E_f[0:1, 3072:3584]
    rsb_bufs = [
        [rsinv_s.bitcast(BF16)[0:1, i * 512:(i + 1) * 512] for i in range(4)],
        [E_b[0:1, 6144:6656], E_b[0:1, 6656:7168],
         E_b[0:1, 8192:8704], E_b[0:1, 8704:9216]],
    ]
    grow_v = E_b[0:1, 8192:8192 + 2 * SW]
    wtproj_v = lambda buf: E_b[:, 4096 + buf * 1024: 4096 + (buf + 1) * 1024]

    def regs_of(c0_bf, c1_bf):
        f0, f1 = c0_bf // 2, (c1_bf + 1) // 2
        return list(range(f0 // 512, min((f1 + 511) // 512, NT * 2)))

    dbg_toks = {}

    def dump(name, src_ap, shape, dtype, res_list=()):
        if not dbg or (isinstance(dbg, (set, frozenset, list, tuple)) and name not in dbg):
            return
        dd = nc.dram_tensor("dbg_" + name, shape, dtype, kind="ExternalOutput").ap()
        bar = [(cs, P.cnt.get(cs, 0)) for cs in ("cPE", "cACT", "cDVE", "cPOOL")]
        for q in ("SP", "POOL", "ACT"):
            bar += P.qbar(q)
        tok = P.dma("POOL", dd, src_ap, deps=[b for b in bar if b[1] > 0])
        for r in res_list:
            r.note_r(tok)
        dbg_toks[name] = tok

    t_idf = P.dma("SP", identf_s[:], identf_d)
    t_idb = P.dma("SP", identb_s[:], identb_d)
    t_onb = P.dma("SP", onesb_s[:], onesb_d)
    t_al = P.dma("SP", alpha_s[:], alpha_d)
    t_rs = P.dma("SP", rsc_s[:], rsc_d)
    t_ep = P.op("DVE", lambda e: e.memset(eps_s[:], 1e-5))
    t_on = P.op("DVE", lambda e: e.memset(one_s[:], 1.0))
    t_oma = P.op("DVE", lambda e: e.tensor_tensor(out=oma_s[:], in0=one_s[:], in1=alpha_s[:],
                                                  op=ALU.subtract), deps=[t_al, t_on])
    P.op("DVE", lambda e: e.memset(mk_s[0:1, 0:64], 1.0))
    P.op("DVE", lambda e: e.memset(mk_s[0:1, 64:192], 0.0))
    t_mk = P.op("DVE", lambda e: e.memset(mk_s[0:1, 192:256], 1.0))
    t_b1f = P.dma("SP", E_f[0:8, 3072:3584], b1_d)
    t_bs1f = P.dma("SP", E_f[8:9, 3072:3584], bs1_d.rearrange("(a b) -> a b", a=1))
    r_wts[0].note_w(t_bs1f)
    tb_all = []
    for hc in range(NHC):
        tt_tp = P.op("PE", lambda e, hc=hc: e.matmul(
            PS[0][:, hc * 16:hc * 16 + 9], E_f[0:9, 3072 + hc * 128:3072 + (hc + 1) * 128],
            identf_s[0:9, 0:9], is_transpose=True, start=True, stop=True),
            deps=[t_b1f, t_bs1f, t_idf] + (r_bank[0].wdep() if hc == 0 else []))
        r_bank[0].note_w(tt_tp)
        r_wts[0].note_r(tt_tp)
        t_c1 = P.op("DVE", lambda e, hc=hc: e.tensor_copy(
            out=b1_s[:, hc * 9:hc * 9 + 9],
            in_=PS[0][:, hc * 16:hc * 16 + 9]), deps=[tt_tp])
        r_bank[0].note_r(t_c1)
        tb_all.append(t_c1)
    t_b2f = P.dma("SP", E_f[0:E, 0:D], b2_d)
    t_b2 = P.op("DVE", lambda e: e.tensor_copy(out=b2row_s[:], in_=E_f[0:E, 0:D]), deps=[t_b2f])
    t_bs2f = P.dma("SP", E_f[0:1, D:2 * D], bs2_d.rearrange("(a b) -> a b", a=1))
    t_bs2 = P.op("DVE", lambda e: e.tensor_scalar(out=bs2row_s[:], in0=E_f[0:1, D:2 * D],
                                                  scalar1=0.5, scalar2=None, op0=ALU.mult),
                 deps=[t_bs2f])
    t_gwf = P.dma("SP", E_f[:, 2048:2048 + NK * E].rearrange("p (k e) -> p k e", k=NK),
                  gw_d.rearrange("(k p) e -> p k e", p=128))
    t_gw = P.op("DVE", lambda e: e.tensor_copy(out=gateW_s[:], in_=E_f[:, 2048:2048 + NK * E]),
                deps=[t_gwf])
    r_stage[0].note_w(t_b2f)
    r_stage[0].note_r(t_b2)
    r_stage[1].note_w(t_b2f)
    r_stage[1].note_r(t_b2)
    r_stage[2].note_w(t_bs2f)
    r_stage[2].note_r(t_bs2)
    r_stage[3].note_w(t_bs2f)
    r_stage[3].note_r(t_bs2)
    r_wrow[0].note_w(t_gwf)
    r_wrow[0].note_r(t_gw)

    for tt in range(NT):
        tok = P.dma(SQ[tt % 3], x_t(tt), x_d[tt * 128:(tt + 1) * 128, :])
        r_x[2 * tt].note_w(tok)
        r_x[2 * tt + 1].note_w(tok)

    xnT_cur = {}

    def emit_ln(park=False, tile_war=None):
        war = r_xnT.wdep() if tile_war is None else []
        t7 = None
        xnT_cur.clear()
        for tt in range(NT):
            sb = tt % 3
            st = stat_s[sb]
            dep_x = r_x[2 * tt].rdep() + r_x[2 * tt + 1].rdep()
            t1 = P.op("DVE", lambda e, tt=tt, st=st: e.bn_stats(out=st[:, 0:6], in_=A_f[:, tt * D: tt * D + 512]),
                      deps=dep_x + r_stat[sb].wdep())
            t2 = P.op("DVE", lambda e, tt=tt, st=st: e.bn_stats(out=st[:, 6:12], in_=A_f[:, tt * D + 512: tt * D + 1024]),
                      deps=dep_x)
            t3 = P.op("DVE", lambda e, st=st: e.bn_aggr(out=st[:, 12:14], in_=st[:, 0:12]),
                      deps=[t1, t2])
            t4 = P.op("ACT", lambda e, st=st: e.activation(out=st[:, 14:15], in_=st[:, 13:14],
                                                           func=AF.Sqrt, bias=eps_s[:]),
                      deps=[t3, t_ep])
            t5 = P.op("DVE", lambda e, st=st: e.reciprocal(out=st[:, 15:16], in_=st[:, 14:15]),
                      deps=[t4])
            xp = tt % 2
            xb = xn_s[xp]
            t6 = P.op("DVE", lambda e, tt=tt, st=st, xb=xb: e.tensor_scalar(
                out=xb[:], in0=A_f[:, tt * D:(tt + 1) * D], scalar1=st[:, 12:13],
                scalar2=st[:, 15:16], op0=ALU.subtract, op1=ALU.mult),
                deps=[t5] + r_xn[xp].wdep())
            r_xn[xp].note_w(t6)
            r_stat[sb].note_w(t5)
            r_stat[sb].note_r(t6)
            r_x[2 * tt].note_r(t6)
            r_x[2 * tt + 1].note_r(t6)
            if park:
                tokp = P.dma("POOL", xpark_d[tt * 128:(tt + 1) * 128, :], x_t(tt), deps=[t6])
                r_x[2 * tt].note_r(tokp)
                r_x[2 * tt + 1].note_r(tokp)
                r_park[tt].note_w(tokp)
            lnb = 7 - (tt % 2)
            bank = r_bank[lnb]
            psv = PS_bf[3][:, 1024:2048] if lnb == 7 else PS_bf[3][:, 0:1024]
            mm_dep = [t6, t_idb] + bank.wdep()
            tokm = None
            for k in range(NK):
                tokm = P.op("PE", lambda e, k=k, xb=xb, psv=psv: e.matmul(
                    psv[:, k * 128:(k + 1) * 128],
                    xb[:, k * 128:(k + 1) * 128], identb_s[:],
                    is_transpose=True, start=(k == 0), stop=(k == NK - 1)),
                    deps=(mm_dep if k == 0 else []), sig=(k == NK - 1))
            r_xn[xp].note_r(tokm)
            bank.note_w(tokm)
            dst = B_b.rearrange("p (k t j) -> p t k j", k=NK, t=NT, j=128)[:, tt]
            t7 = P.op("DVE", lambda e, dst=dst, psv=psv: e.tensor_copy(out=dst, in_=psv),
                      deps=[tokm] + (war if tile_war is None else [tile_war[tt]]))
            bank.note_r(t7)
            xnT_cur[tt] = t7
        r_xnT.note_w(t7)

    gtoks = {}
    gftoks = {}
    gf_defer = []

    def emit_gate():
        t_gf = None
        for s in range(NS):
            b6 = r_bank[6]
            deps = b6.wdep() + [xnT_cur[s * NST + NST - 1], t_gw]
            tokm = None
            for k in range(NK):
                tokm = P.op("PE", lambda e, k=k, s=s: e.matmul(
                    PSH[6][0:E, 0:SW], gateW_s[:, k * E:(k + 1) * E],
                    xnT_ch(k)[:, s * SW:(s + 1) * SW], start=(k == 0), stop=(k == NK - 1)),
                    deps=(deps if k == 0 else []), sig=(k == NK - 1))
            r_xnT.note_r(tokm)
            b6.note_w(tokm)
            t_sig = P.op("ACT", lambda e: e.activation(out=gfm_s[:], in_=PSH[6][0:E, 0:SW],
                                                       func=AF.Sigmoid),
                         deps=[tokm] + r_gfm.wdep())
            b6.note_r(t_sig)
            t_grs = P.op("DVE", lambda e: e.tensor_scalar(out=gfm_s[:], in0=gfm_s[:],
                                                          scalar1=rsc_s[0:E, :], scalar2=None,
                                                          op0=ALU.mult),
                         deps=[t_sig, t_rs])
            r_gfm.note_w(t_grs)
            for j in range(NST):
                tt = s * NST + j
                p2 = tt % 2
                b7 = r_bank[7]
                tok_tp = P.op("PE", lambda e, j=j: e.matmul(
                    PS[3][:, 512:520], gfm_s[0:E, j * 128:(j + 1) * 128], identf_s[0:E, 0:E],
                    is_transpose=True, start=True, stop=True),
                    deps=[t_grs, t_idf] + b7.wdep())
                b7.note_w(tok_tp)
                r_gfm.note_r(tok_tp)
                t_ev = P.op("DVE", lambda e, tt=tt: e.tensor_copy(
                    out=g_s[:, tt * E:(tt + 1) * E], in_=PS[3][:, 512:520]),
                    deps=[tok_tp] + (r_g.wdep() if tt == 0 else []))
                b7.note_r(t_ev)
                gv = g_s[:, tt * E:(tt + 1) * E]
                sc = gsc_s[:, p2 * 16:p2 * 16 + 16]
                tA = P.op("DVE", lambda e, gv=gv, sc=sc: e.reduce_max(out=sc[:, 8:9], in_=gv, axis=AX.X),
                          deps=[t_ev] + r_gsc[p2].wdep())
                tB = P.op("DVE", lambda e, gv=gv, sc=sc: e.tensor_scalar(
                    out=sc[:, 0:8], in0=gv, scalar1=sc[:, 8:9], scalar2=None, op0=ALU.is_ge),
                    deps=[tA])
                tC = P.op("DVE", lambda e, gv=gv, sc=sc: e.scalar_tensor_tensor(
                    out=sc[:, 0:8], in0=sc[:, 0:8], scalar=-1e30, in1=gv,
                    op0=ALU.mult, op1=ALU.add), deps=[tB])
                tD = P.op("DVE", lambda e, sc=sc: e.reduce_max(out=sc[:, 9:10], in_=sc[:, 0:8], axis=AX.X),
                          deps=[tC])
                tE = P.op("DVE", lambda e, sc=sc: e.tensor_tensor(
                    out=sc[:, 10:11], in0=sc[:, 8:9], in1=sc[:, 9:10], op=ALU.add), deps=[tD])
                tF = P.op("DVE", lambda e, sc=sc: e.reciprocal(out=sc[:, 11:12], in_=sc[:, 10:11]),
                          deps=[tE])
                tG = P.op("DVE", lambda e, sc=sc: e.tensor_scalar(
                    out=sc[:, 12:13], in0=sc[:, 11:12], scalar1=0.5, scalar2=None, op0=ALU.mult),
                    deps=[tF])
                tH = P.op("DVE", lambda e, gv=gv, sc=sc: e.tensor_scalar(
                    out=sc[:, 0:8], in0=gv, scalar1=sc[:, 9:10], scalar2=None, op0=ALU.is_ge),
                    deps=[tD, tC])
                tI = P.op("DVE", lambda e, gv=gv, sc=sc: e.scalar_tensor_tensor(
                    out=gv, in0=sc[:, 0:8], scalar=sc[:, 12:13], in1=gv,
                    op0=ALU.mult, op1=ALU.mult), deps=[tG, tH])
                r_gsc[p2].note_w(tI)
                gtoks[tt] = tI

                def mk_gf(tt=tt, gv=gv, tI=tI):
                    b7b = r_bank[7]
                    tok2 = P.op("PE", lambda e, gv=gv: e.matmul(
                        PS[3][0:E, 640:768], gv, identf_s[:],
                        is_transpose=True, start=True, stop=True),
                        deps=[tI, t_idf] + b7b.wdep())
                    b7b.note_w(tok2)
                    r_g.note_r(tok2)
                    t_gf = P.op("ACT", lambda e, tt=tt: e.activation(
                        out=gfin_s[:, tt * 128:(tt + 1) * 128], in_=PS[3][0:E, 640:768],
                        func=AF.Copy), deps=[tok2] + (r_gfin.wdep() if tt == 0 else []))
                    b7b.note_r(t_gf)
                    gftoks[tt] = t_gf
                    return t_gf
                gf_defer.append(mk_gf)

    park_toks = {}

    def emit_ffn(prep=None, park01=False, pre=None):
        def stream_expert(e_i, buf):
            d1 = r_w1[buf].wdep()
            d2 = r_w2[buf].wdep()
            w1src = w1_d[e_i].rearrange("(k p) h -> p k h", p=128) if e_i < E else \
                    ws1_d.rearrange("(k p) h -> p k h", p=128)
            w2src = w2_d[e_i].rearrange("(k p) h -> p k h", p=128) if e_i < E else \
                    ws2_d.rearrange("(k p) h -> p k h", p=128)
            t_w1 = t_w2 = None
            for i in range(NK):
                sb = stg_slot()
                tl = P.dma(SQ[i % 3], stg(sb), w1src[:, i, :], deps=r_stage[sb].wdep())
                r_stage[sb].note_w(tl)
                t_w1 = P.op("DVE", lambda e, i=i, buf=buf, sb=sb: e.tensor_copy(
                    out=w1v(buf)[:, i * 512:(i + 1) * 512], in_=stg(sb)),
                    deps=[tl] + (d1 if i == 0 else []))
                r_stage[sb].note_r(t_w1)
            for i in range(NK):
                sb = stg_slot()
                hc, j = i // 2, i % 2
                tl = P.dma(SQ[(i + 1) % 3], stg(sb), w2src[:, hc, j * 512:(j + 1) * 512],
                           deps=r_stage[sb].wdep())
                r_stage[sb].note_w(tl)
                if e_i < E:
                    t_w2 = P.op("DVE", lambda e, buf=buf, sb=sb, hc=hc, j=j: e.tensor_copy(
                        out=w2v(buf)[:, hc * D + j * 512: hc * D + (j + 1) * 512], in_=stg(sb)),
                        deps=[tl] + (d2 if i == 0 else []))
                else:
                    t_w2 = P.op("DVE", lambda e, buf=buf, sb=sb, hc=hc, j=j: e.tensor_scalar(
                        out=w2v(buf)[:, hc * D + j * 512: hc * D + (j + 1) * 512], in0=stg(sb),
                        scalar1=0.5, scalar2=None, op0=ALU.mult),
                        deps=[tl] + (d2 if i == 0 else []))
                r_stage[sb].note_r(t_w2)
            r_w1[buf].note_w(t_w1)
            r_w2[buf].note_w(t_w2)
            if park01 and e_i < 2:
                ts1 = P.dma(nxq(), w1bf_d[e_i], w1v(buf), deps=[t_w1])
                ts2 = P.dma(nxq(), w2bf_d[e_i], w2v(buf), deps=[t_w2])
                r_w1[buf].note_r(ts1)
                r_w2[buf].note_r(ts2)
                park_toks[e_i] = (ts1, ts2)
            return t_w1, t_w2

        def get_toks(e_i, buf):
            if pre is not None and e_i in pre:
                return pre[e_i], pre_w2[e_i]
            return stream_expert(e_i, buf)

        def pstep():
            if prep is not None:
                prep.step()

        w_toks = {0: get_toks(0, 0)}
        for e_i in range(E + 1):
            buf = e_i % 2
            t_w1, t_w2 = w_toks.pop(e_i)
            if e_i < E:
                w_toks[e_i + 1] = get_toks(e_i + 1, (e_i + 1) % 2)
            if e_i == 1 and gf_defer:
                t_gf_last = None
                for f in gf_defer:
                    t_gf_last = f()
                gf_defer.clear()
                r_gfin.note_w(t_gf_last)
            hg_war = r_hg[buf].wdep()
            slice_art = {}
            def emit_mm1_slice(s):
                hg_toks = {}
                def mm_group(hc):
                    hb = 4 + (s * NHC + hc) % 2
                    bank = r_bank[hb]
                    deps = bank.wdep() + [t_w1] + ([xnT_cur[s * NST + NST - 1]] if hc == 0 else [])
                    tokm = None
                    for k in range(NK):
                        tokm = P.op("PE", lambda e, k=k, hc=hc, s=s, buf=buf, hb=hb: e.matmul(
                            PSH[hb][:, 0:SW], w1v(buf)[:, k * H + hc * 128: k * H + (hc + 1) * 128],
                            xnT_ch(k)[:, s * SW:(s + 1) * SW], start=(k == 0), stop=(k == NK - 1)),
                            deps=(deps if k == 0 else []), sig=(k == NK - 1))
                    r_xnT.note_r(tokm)
                    r_w1[buf].note_r(tokm)
                    bank.note_w(tokm)
                    bcol = hc * 9 + (e_i if e_i < E else 8)
                    t_hg = P.op("ACT", lambda e, hc=hc, s=s, buf=buf, hb=hb, bcol=bcol: e.activation(
                        out=hg_v(buf, hc)[:, s * SW:(s + 1) * SW], in_=PSH[hb][:, 0:SW],
                        func=AF.Gelu, bias=b1_s[:, bcol: bcol + 1]),
                        deps=[tokm] + tb_all + (hg_war if (s == 0 and hc == 0) else []))
                    bank.note_r(t_hg)
                    hg_toks[hc] = t_hg
                for hc in range(NHC):
                    mm_group(hc)
                slice_art[s] = hg_toks

            def emit_mm2_slice(s):
                hg_toks = slice_art[s]
                for j in range(NST):
                    tt = s * NST + j
                    for dc in range(2):
                        yb = (tt * 2 + dc) % 4
                        bank = r_bank[yb]
                        n_mm = NHC + (2 if e_i == E else 0)
                        deps = bank.wdep() + [hg_toks[h2] for h2 in range(NHC)] + [t_w2]
                        im = 0
                        tokm = None
                        for hc in range(NHC):
                            im += 1
                            tokm = P.op("PE", lambda e, hc=hc, tt=tt, dc=dc, buf=buf, yb=yb, im=im, n_mm=n_mm: e.matmul(
                                PSH[yb][:, 0:512], hg_v(buf, hc)[:, tt * 128:(tt + 1) * 128],
                                w2v(buf)[:, hc * D + dc * 512: hc * D + (dc + 1) * 512],
                                start=(im == 1), stop=(im == n_mm)),
                                deps=(deps if im == 1 else []), sig=(im == n_mm))
                        if e_i == E:
                            P.op("PE", lambda e, tt=tt, dc=dc, yb=yb: e.matmul(
                                PSH[yb][:, 0:512], gfin_s[:, tt * 128:(tt + 1) * 128],
                                b2row_s[0:E, dc * 512:(dc + 1) * 512], start=False, stop=False),
                                deps=[t_b2, gftoks[tt]] + r_gfin.rdep(), sig=False)
                            tokm = P.op("PE", lambda e, tt=tt, dc=dc, yb=yb: e.matmul(
                                PSH[yb][:, 0:512], onesb_s[:],
                                bs2row_s[0:1, dc * 512:(dc + 1) * 512], start=False, stop=True),
                                deps=[t_bs2, t_onb], sig=True)
                            r_gfin.note_r(tokm)
                        r_hg[buf].note_r(tokm)
                        r_w2[buf].note_r(tokm)
                        bank.note_w(tokm)
                        reg = r_x[2 * tt + dc]
                        if e_i < E:
                            t_add = P.op("DVE", lambda e, tt=tt, dc=dc, yb=yb, e_i=e_i: e.scalar_tensor_tensor(
                                out=x_h(tt, dc), in0=PSH[yb][:, 0:512],
                                scalar=g_s[:, tt * E + e_i: tt * E + e_i + 1],
                                in1=x_h(tt, dc), op0=ALU.mult, op1=ALU.add),
                                deps=[tokm, gtoks[tt]] + reg.wdep())
                            r_g.note_r(t_add)
                        else:
                            t_add = P.op("DVE", lambda e, tt=tt, dc=dc, yb=yb: e.tensor_tensor(
                                out=x_h(tt, dc), in0=x_h(tt, dc), in1=PSH[yb][:, 0:512], op=ALU.add),
                                deps=[tokm] + reg.wdep())
                        bank.note_r(t_add)
                        reg.note_w(t_add)

            emit_mm1_slice(0)
            pstep()
            for s in range(1, NS):
                emit_mm1_slice(s)
                pstep()
                emit_mm2_slice(s - 1)
                pstep()
            emit_mm2_slice(NS - 1)
            pstep()


    class PrepSched:
        def __init__(self, jobs):
            self.jobs = list(jobs)
            self.li = 0
            self.ti = 0
            self.art = {}

        def _load(self, i):
            w, m = self.jobs[i]
            wb = i % 3
            tcs = []
            for h2 in range(2):
                sb = stg_slot()
                tl = P.dma(nxq(), stg(sb), watt_d[w][m * 128:(m + 1) * 128, h2 * 512:(h2 + 1) * 512],
                           deps=r_stage[sb].wdep())
                r_stage[sb].note_w(tl)
                tc = P.op("DVE", lambda e, sb=sb, wb=wb, h2=h2: e.tensor_copy(
                    out=wrow_v(wb)[:, h2 * 512:(h2 + 1) * 512], in_=stg(sb)),
                    deps=[tl] + (r_wrow[wb].wdep() if h2 == 0 else []))
                r_stage[sb].note_r(tc)
                tcs.append(tc)
            r_wrow[wb].note_w(tcs[-1])
            self.art[i] = tcs

        def _tp(self, i):
            w, m = self.jobs[i]
            wb = i % 3
            wsb = i % 2
            tcs = self.art.pop(i)
            tokt = None
            for k in range(NK):
                tokt = P.op("PE", lambda e, k=k, wb=wb: e.matmul(
                    PS_bf[3][:, k * 128:(k + 1) * 128], wrow_v(wb)[:, k * 128:(k + 1) * 128],
                    identb_s[:], is_transpose=True, start=True, stop=True),
                    deps=[tcs[k // 4], t_idb] + (r_bank[6].wdep() if k == 0 else []))
            r_bank[6].note_w(tokt)
            r_wrow[wb].note_r(tokt)
            t_ev = P.op("DVE", lambda e, wsb=wsb: e.tensor_copy(
                out=wts_v(wsb)[:, 0:D], in_=PS_bf[3][:, 0:1024]),
                deps=[tokt] + r_wts[wsb].wdep())
            r_bank[6].note_r(t_ev)
            r_wts[wsb].note_w(t_ev)
            if w < 4:
                ts = P.dma(nxq(), wtbf_d[w, m], wts_v(wsb)[:, 0:D], deps=[t_ev])
                r_wts[wsb].note_r(ts)
            else:
                for k in range(NK):
                    ts = P.dma(nxq(), wotbf_d[k][:, m * 128:(m + 1) * 128],
                               wts_v(wsb)[:, k * 128:(k + 1) * 128], deps=[t_ev])
                    r_wts[wsb].note_r(ts)

        def step(self):
            if self.ti < len(self.jobs) and self.ti <= self.li - 2:
                self._tp(self.ti)
                self.ti += 1
            if self.li < len(self.jobs):
                self._load(self.li)
                self.li += 1

        def flush(self):
            while self.ti < len(self.jobs):
                if self.li < len(self.jobs):
                    self._load(self.li)
                    self.li += 1
                self._tp(self.ti)
                self.ti += 1

    emit_ln(park=False)
    dump("xnT1", B_b[:], [128, NK * T], BF16, [r_xnT])
    dump("b1s", b1_s[:], [128, 40], F32, [])
    emit_gate()
    dump("gfin1", gfin_s[:], [E, T], BF16, [r_gfin])
    prep = PrepSched([(w, m) for w in range(5) for m in range(NK)])
    emit_ffn(prep=prep, park01=True)
    prep.flush()
    dump("x1", A_f[:], [128, NT * D], F32, [r_x[i] for i in range(2 * NT)])

    emit_ln(park=True)

    t_vones = P.op("DVE", lambda e: e.memset(C_b[:, 64:HEADS * NC * 66:66], 1.0),
                   deps=r_hg[0].wdep() + r_hg[1].wdep() + r_vones.wdep())
    t_vpad = P.op("DVE", lambda e: e.memset(C_b[:, 65:HEADS * NC * 66:66], 0.0),
                  deps=[t_vones])
    r_vones.note_w(t_vpad)
    kD_war = r_w1[0].wdep() + r_w1[1].wdep() + r_w2[0].wdep() + r_w2[1].wdep()
    for pt in (P.qbar("SP") + P.qbar("POOL") + P.qbar("ACT") + P.qbar("DVE") +
               [(cs, P.cnt.get(cs, 0)) for cs in ("cPE", "cACT", "cDVE", "cPOOL")]):
        r_wt[0].note_r(pt)
        r_wt[1].note_r(pt)
        r_vfm[0].note_r(pt)
        r_vfm[1].note_r(pt)
    vaug_toks = {}
    proj_last = {}
    for w in (0, 1, 2, 3):
        name = ("q", "k", "v", "kp")[w]
        for m in range(NK):
            wb = m % 4
            t_wt = P.dma(("ACT", "SP")[m % 2], wtproj_v(wb)[:, 0:D], wtbf_d[w, m], deps=r_wt[wb].wdep())
            r_wt[wb].note_w(t_wt)
            for s in range(NS):
                pb = 4 + s % 4 if NS >= 4 else 4 + s % 2
                bank = r_bank[pb]
                deps = bank.wdep() + [t_wt, xnT_cur[s * NST + NST - 1]]
                tokm = None
                for k in range(NK):
                    tokm = P.op("PE", lambda e, k=k, s=s, wb=wb, pb=pb: e.matmul(
                        PSH[pb][:, 0:SW], wtproj_v(wb)[:, k * 128:(k + 1) * 128],
                        xnT_ch(k)[:, s * SW:(s + 1) * SW], start=(k == 0), stop=(k == NK - 1)),
                        deps=(deps if k == 0 else []), sig=(k == NK - 1))
                r_xnT.note_r(tokm)
                r_wt[wb].note_r(tokm)
                bank.note_w(tokm)
                if name == "v":
                    vb = m % 2
                    t_ev = P.op("ACT", lambda e, s=s, vb=vb, pb=pb: e.activation(
                        out=vfm_v(vb)[:, s * SW:(s + 1) * SW], in_=PSH[pb][:, 0:SW], func=AF.Copy),
                        deps=[tokm] + (r_vfm[vb].wdep() if s == 0 else []))
                    bank.note_r(t_ev)
                    if s == NS - 1:
                        r_vfm[vb].note_w(t_ev)
                else:
                    if name == "q":
                        dst = q_ch(m)
                        war = [d for i in regs_of(m * T, (m + 1) * T) for d in r_x[i].wdep()] if s == 0 else []
                    elif name == "kp":
                        dst = kp_ch(m)
                        war = [d for i in regs_of((NK + m) * T, (NK + m + 1) * T) for d in r_x[i].wdep()] if s == 0 else []
                    else:
                        dst = k_ch(m)
                        war = kD_war if (m == 0 and s == 0) else []
                    if s % 2 == 0:
                        t_ev = P.op("ACT", lambda e, dst=dst, s=s, pb=pb: e.activation(
                            out=dst[:, s * SW:(s + 1) * SW], in_=PSH[pb][:, 0:SW], func=AF.Copy),
                            deps=[tokm] + war)
                    else:
                        t_ev = P.op("DVE", lambda e, dst=dst, s=s, pb=pb: e.tensor_copy(
                            out=dst[:, s * SW:(s + 1) * SW], in_=PSH[pb][:, 0:SW]),
                            deps=[tokm] + war)
                    bank.note_r(t_ev)
                    proj_last.setdefault((name, m), []).append(t_ev)
                    if name == "q":
                        for i in regs_of(m * T, (m + 1) * T):
                            r_x[i].note_w(t_ev)
                    elif name == "kp":
                        for i in regs_of((NK + m) * T, (NK + m + 1) * T):
                            r_x[i].note_w(t_ev)
                    else:
                        r_kD.note_w(t_ev)
            if name == "v":
                vb = m % 2
                vtok = r_vfm[vb].rdep()
                tok = None
                for c in range(NC):
                    j = c % 4
                    pbank = r_bank[j]
                    psv = PS_bf[j >> 1][:, (j & 1) * 1024:(j & 1) * 1024 + 128]
                    tokt = P.op("PE", lambda e, c=c, vb=vb, psv=psv: e.matmul(
                        psv, vfm_v(vb)[:, c * 128:(c + 1) * 128], identb_s[:],
                        is_transpose=True, start=True, stop=True),
                        deps=vtok + [t_idb] + pbank.wdep())
                    pbank.note_w(tokt)
                    r_vfm[vb].note_r(tokt)
                    for hh in range(2):
                        h = 2 * m + hh
                        tok = P.op("DVE", lambda e, h=h, c=c, hh=hh, psv=psv: e.tensor_copy(
                            out=vaug(h, c)[:, 0:64], in_=psv[:, hh * 64:(hh + 1) * 64]),
                            deps=[tokt] + (r_vaug.wdep() if (m == 0 and c == 0 and hh == 0) else []) +
                                 ([t_vones] if (m == 0 and c == 0 and hh == 0) else []))
                        vaug_toks[(h, c)] = tok
                        pbank.note_r(tok)
                r_vaug.note_w(tok)

    dump("qkp", A_b[:], [128, 2 * NK * T], BF16, [r_x[i] for i in range(2 * NT)])
    dump("kch", D_b[:, 0:NK * T], [128, NK * T], BF16, [r_kD])
    dump("vaug", C_b[:, 0:HEADS * NC * 66], [128, HEADS * NC * 66], BF16, [r_vaug, r_vones])
    dump("xnT2", B_b[:], [128, NK * T], BF16, [r_xnT])
    for pt in (P.qbar("SP") + P.qbar("POOL") + P.qbar("ACT") + P.qbar("DVE") +
               [(cs, P.cnt.get(cs, 0)) for cs in ("cPE", "cACT", "cDVE", "cPOOL")]):
        for rr in [r_A[0], r_A[1], r_A[2], r_A[3], r_t12, r_scr,
                   r_rsinv[0], r_rsinv[1]] + list(r_oun):
            rr.note_r(pt)
    mix_last = None
    pend = [None]
    rsbi = [0]
    nonlocal_mix = [None]
    pre_w1 = {}
    pre_w2 = {}
    for pair in range(NPAIR):
        hA, hB = 2 * pair, 2 * pair + 1
        q_src = q_ch(pair)
        mix0 = {}
        for br in range(2):
            kv = k_ch(pair) if br == 0 else kp_ch(pair)
            kdeps = proj_last[("k" if br == 0 else "kp", pair)]
            qdeps = proj_last[("q", pair)]
            for s in range(NS):
                depO = r_bank[4].wdep() + r_bank[5].wdep()
                t_oa = t_ob = None
                prevq = []
                t_oa = t_ob = None
                def emit_O(c, abuf):
                    nonlocal t_oa, t_ob
                    dA = [vaug_toks[(hA, c)]] + r_A[abuf].rdep() + ([t_vones] if c == 0 else []) + (depO if c == 0 else [])
                    t_oa = P.op("PE", lambda e, c=c, hA=hA, abuf=abuf: e.matmul(
                        PSH[4][0:65, 0:SW], vaug(hA, c), A_v(abuf)[:, 0:SW],
                        start=(c == 0), stop=(c == NC - 1)),
                        deps=dA, sig=(c == NC - 1))
                    t_ob = P.op("PE", lambda e, c=c, hB=hB, abuf=abuf: e.matmul(
                        PSH[5][0:65, 0:SW], vaug(hB, c), A_v(abuf)[:, 512:512 + SW],
                        start=(c == 0), stop=(c == NC - 1)),
                        deps=[vaug_toks[(hB, c)]], sig=(c == NC - 1))
                    r_A[abuf].note_r(t_ob)
                for c in range(NC):
                    sb = c % 2
                    abuf = c % 4
                    bankA, bankB = r_bank[2 * sb], r_bank[2 * sb + 1]
                    d0 = bankA.wdep() + bankB.wdep() + kdeps + qdeps
                    t_sa = P.op("PE", lambda e, c=c, s=s, sb=sb, kv=kv, q_src=q_src: e.matmul(
                        PS[sb][:, 0:SW], kv[0:64, c * 128:(c + 1) * 128],
                        q_src[0:64, s * SW:(s + 1) * SW], start=True, stop=True), deps=d0)
                    t_sb_ = P.op("PE", lambda e, c=c, s=s, sb=sb, kv=kv, q_src=q_src: e.matmul(
                        PS[sb][:, 512:512 + SW], kv[64:128, c * 128:(c + 1) * 128],
                        q_src[64:128, s * SW:(s + 1) * SW], start=True, stop=True), deps=[])
                    bankA.note_w(t_sa)
                    bankB.note_w(t_sb_)
                    ab = r_A[abuf]
                    t_exp = P.op("ACT", lambda e, sb=sb, abuf=abuf: e.activation(
                        out=A_v(abuf), in_=PS[sb][:, 0:1024], func=AF.Exp, scale=0.125),
                        deps=[t_sa, t_sb_] + ab.wdep())
                    bankA.note_r(t_exp)
                    bankB.note_r(t_exp)
                    ab.note_w(t_exp)
                    prevq.append((c, abuf))
                    if len(prevq) > 2:
                        emit_O(*prevq.pop(0))
                    if c == 8 and pend[0] is not None:
                        pend[0]()
                        pend[0] = None
                for item in prevq:
                    emit_O(*item)
                prevq = []
                slot = r_oun[br * NS + s]
                ev_a = P.op("DVE", lambda e, br=br, s=s: e.tensor_copy(
                    out=oun_v(br, s)[0:64, :], in_=PSH[4][0:64, 0:SW]),
                    deps=[t_oa] + slot.wdep())
                ev_b = P.op("DVE", lambda e, br=br, s=s: e.tensor_copy(
                    out=oun_v(br, s)[64:128, :], in_=PSH[5][0:64, 0:SW]),
                    deps=[t_ob])
                r_bank[4].note_r(ev_a)
                r_bank[5].note_r(ev_b)
                slot.note_w(ev_b)
                rb = rsbi[0] % 2
                rsbi[0] += 1
                rsb = rsb_bufs[rb]
                r_rb = r_rsinv[rb]
                rs_a = P.op("DVE", lambda e: e.tensor_copy(out=s2row_s[:], in_=PSH[4][64:65, 0:SW]),
                            deps=[t_oa] + r_s2row.wdep())
                rs_b = P.op("DVE", lambda e: e.tensor_copy(out=rdrow_s[:], in_=PSH[5][64:65, 0:SW]),
                            deps=[t_ob] + r_rdrow.wdep())
                r_bank[4].note_r(rs_a)
                r_bank[5].note_r(rs_b)
                r_s2row.note_w(rs_a)
                r_rdrow.note_w(rs_b)
                rcA = P.op("DVE", lambda e: e.reciprocal(out=s2row_s[:], in_=s2row_s[:]),
                           deps=[rs_a])
                hiA = P.op("DVE", lambda e, rsb=rsb: e.tensor_copy(out=rsb[0], in_=s2row_s[:]),
                           deps=[rcA] + r_rb.wdep())
                loA = P.op("DVE", lambda e, rsb=rsb: e.tensor_tensor(out=rsb[1], in0=s2row_s[:],
                                                                     in1=rsb[0], op=ALU.subtract),
                           deps=[hiA])
                rcB = P.op("DVE", lambda e: e.reciprocal(out=rdrow_s[:], in_=rdrow_s[:]),
                           deps=[rs_b])
                hiB = P.op("DVE", lambda e, rsb=rsb: e.tensor_copy(out=rsb[2], in_=rdrow_s[:]),
                           deps=[rcB])
                loB = P.op("DVE", lambda e, rsb=rsb: e.tensor_tensor(out=rsb[3], in0=rdrow_s[:],
                                                                     in1=rsb[2], op=ALU.subtract),
                           deps=[hiB])
                r_s2row.note_r(loA)
                r_rdrow.note_r(loB)
                r_rb.note_w(loB)

                def norm_mix(pair=pair, br=br, s=s, slot=slot, ev_b=ev_b,
                             loA=loA, loB=loB, rsb=rsb, r_rb=r_rb, mix0=mix0):
                    bbank = r_bank[6 + br]
                    t_bc = None
                    for bi, mc in enumerate((0, 0, 128, 128)):
                        t_bc = P.op("PE", lambda e, br=br, mc=mc, bi=bi, rsb=rsb: e.matmul(
                            PSH[6 + br][:, 0:SW], mk_s[0:1, mc:mc + 128], rsb[bi],
                            start=(bi == 0), stop=(bi == 3)),
                            deps=(([loA, loB, t_mk] + bbank.wdep()) if bi == 0 else []), sig=(bi == 3))
                    bbank.note_w(t_bc)
                    r_rb.note_r(t_bc)
                    if br == 0:
                        m0 = P.op("DVE", lambda e, pair=pair, s=s: e.scalar_tensor_tensor(
                            out=omix_ch(pair)[:, s * SW:(s + 1) * SW], in0=oun_v(0, s)[:, :],
                            scalar=alpha_s[:], in1=PSH[6][:, 0:SW], op0=ALU.mult, op1=ALU.mult),
                            deps=[t_bc, t_al, ev_b] +
                                 (r_xnT.wdep() if (pair == 0 and s == 0) else []))
                        bbank.note_r(m0)
                        slot.note_r(m0)
                        mix0[s] = m0
                    else:
                        m1 = P.op("DVE", lambda e, s=s: e.scalar_tensor_tensor(
                            out=t2_s, in0=oun_v(1, s)[:, :], scalar=oma_s[:], in1=PSH[7][:, 0:SW],
                            op0=ALU.mult, op1=ALU.mult),
                            deps=[t_bc, t_oma, ev_b] + r_t12.wdep())
                        bbank.note_r(m1)
                        slot.note_r(m1)
                        mm = P.op("DVE", lambda e, pair=pair, s=s: e.tensor_tensor(
                            out=omix_ch(pair)[:, s * SW:(s + 1) * SW],
                            in0=omix_ch(pair)[:, s * SW:(s + 1) * SW], in1=t2_s, op=ALU.add),
                            deps=[m1, mix0[s]])
                        r_t12.note_w(mm)
                        nonlocal_mix[0] = mm
                pend[0] = norm_mix
            if br == 0:
                r_kD.note_r(t_sb_)
            for i in regs_of(pair * T, (pair + 1) * T):
                r_x[i].note_r(t_sb_)
            for i in regs_of((NK + pair) * T, (NK + pair + 1) * T):
                r_x[i].note_r(t_sb_)
        for tt2 in range(NT):
            f0, f1 = tt2 * D, (tt2 + 1) * D
            b0, b1 = 2 * f0, 2 * f1
            needed = set()
            for m in range(NK):
                if m * T < b1 and (m + 1) * T > b0:
                    needed.add(m)
                km0, km1 = NK * T + m * T, NK * T + (m + 1) * T
                if km0 < b1 and km1 > b0:
                    needed.add(m)
            if needed and max(needed) == pair:
                tokr = P.dma("POOL", x_t(tt2), xpark_d[tt2 * 128:(tt2 + 1) * 128, :],
                             deps=r_park[tt2].rdep() + r_x[2 * tt2].wdep() + r_x[2 * tt2 + 1].wdep())
                r_x[2 * tt2].note_w(tokr)
                r_x[2 * tt2 + 1].note_w(tokr)
        if pair in (1, 3):
            pe_i = 0 if pair == 1 else 1
            tok_pre = P.dma(nxq(), w1v(pe_i), w1bf_d[pe_i],
                            deps=r_w1[pe_i].wdep() + r_kD.wdep() + [park_toks[pe_i][0]])
            r_w1[pe_i].note_w(tok_pre)
            r_kD.note_w(tok_pre)
            pre_w1[pe_i] = tok_pre
    if pend[0] is not None:
        pend[0]()
        pend[0] = None
    mix_last = nonlocal_mix[0]
    r_xnT.note_w(mix_last)

    dump("omix", B_b[:], [128, NK * T], BF16, [r_xnT])
    t_wo = {}
    wo_bar = P.qbar("SP") + P.qbar("POOL") + P.qbar("DVE")
    for dc in range(NK):
        t_wo[dc] = P.dma("ACT", wo_ch(dc), wotbf_d[dc],
                         deps=(r_kD.wdep() + wo_bar if dc == 0 else []))
        r_kD.note_w(t_wo[dc])
    wo_tile = {}
    for tt in range(NT):
        for cs in range(2):
            yb = (tt * 2 + cs) % 4
            bank = r_bank[yb]
            deps = bank.wdep() + [t_wo[NK - 1], mix_last]
            tokm = None
            for dc in range(NK):
                tokm = P.op("PE", lambda e, dc=dc, tt=tt, cs=cs, yb=yb: e.matmul(
                    PSH[yb][:, 0:512], omix_ch(dc)[:, tt * 128:(tt + 1) * 128],
                    wo_ch(dc)[:, cs * 512:(cs + 1) * 512], start=(dc == 0), stop=(dc == NK - 1)),
                    deps=([t_wo[dc]] + (deps if dc == 0 else [])), sig=(dc == NK - 1))
            r_xnT.note_r(tokm)
            r_kD.note_r(tokm)
            bank.note_w(tokm)
            wo_tile[tt] = tokm
            reg = r_x[2 * tt + cs]
            t_add = P.op("DVE", lambda e, tt=tt, cs=cs, yb=yb: e.tensor_tensor(
                out=x_h(tt, cs), in0=x_h(tt, cs), in1=PSH[yb][:, 0:512], op=ALU.add),
                deps=[tokm] + reg.wdep())
            bank.note_r(t_add)
            reg.note_w(t_add)

    for rr in (r_w1[0], r_w1[1], r_w2[0], r_w2[1]):
        rr.note_r(tokm)
    for _e in (0, 1):
        _t = P.dma(nxq(), w2v(_e), w2bf_d[_e],
                   deps=r_w2[_e].wdep() + [park_toks[_e][1]])
        r_w2[_e].note_w(_t)
        pre_w2[_e] = _t

    dump("x2", A_f[:], [128, NT * D], F32, [r_x[i] for i in range(2 * NT)])

    emit_ln(park=False, tile_war=wo_tile)
    emit_gate()
    ffn2_bar = ([(cs, P.cnt.get(cs, 0)) for cs in ("cPE", "cACT", "cDVE", "cPOOL")] +
                P.qbar("SP") + P.qbar("ACT") + P.qbar("POOL"))
    for _i in range(NSTG):
        for _b in ffn2_bar:
            r_stage[_i].note_r(_b)
    emit_ffn(pre=pre_w1)

    t_out = None
    for tt in range(NT):
        t_out = P.dma(SQ[tt % 3], out_d[tt * 128:(tt + 1) * 128, :], x_t(tt),
                      deps=r_x[2 * tt].rdep() + r_x[2 * tt + 1].rdep())
        r_x[2 * tt].note_r(t_out)
        r_x[2 * tt + 1].note_r(t_out)
    P.op("POOL", lambda e: e.engine_nop(),
         deps=P.qbar("POOL") + P.qbar("SP") + P.qbar("ACT"), sig=False)

    sem_names = ["cPE", "cACT", "cDVE", "cPOOL"]
    for qe in ("SP", "POOL", "ACT", "DVE"):
        for i in range(Plan.NDMA):
            sem_names.append(f"d{qe}{i}")
    import contextlib
    with contextlib.ExitStack() as stack:
        sems = {n: stack.enter_context(nc.semaphore(n)) for n in sem_names}
        with nc.Block() as block:
            @block.tensor
            def _(eng):
                P.emit("PE", eng, sems)
            @block.scalar
            def _(eng):
                P.emit("ACT", eng, sems)
            @block.vector
            def _(eng):
                P.emit("DVE", eng, sems)
            @block.gpsimd
            def _(eng):
                P.emit("POOL", eng, sems)
            @block.sync
            def _(eng):
                P.emit("SP", eng, sems)
    return nc


def kernel(**inputs):
    from concourse.bass_utils import run_bass_kernel_spmd
    T = 2048
    x = np.ascontiguousarray(np.asarray(inputs["L0_attn_x"], dtype=np.float32))
    B = x.shape[0]
    f32 = np.float32
    alpha_bc = np.broadcast_to(np.asarray(inputs["L0_attn_alpha"], dtype=f32).reshape(1, 1), (128, 1)).copy()
    rs_bc = np.broadcast_to(np.asarray(inputs["L0_ffn_route_scale"], dtype=f32).reshape(1, 1), (128, 1)).copy()
    common = {
        "W_Q": np.ascontiguousarray(inputs["L0_attn_W_Q"], dtype=f32),
        "W_K": np.ascontiguousarray(inputs["L0_attn_W_K"], dtype=f32),
        "W_V": np.ascontiguousarray(inputs["L0_attn_W_V"], dtype=f32),
        "W_K_prime": np.ascontiguousarray(inputs["L0_attn_W_K_prime"], dtype=f32),
        "W_O": np.ascontiguousarray(inputs["L0_attn_W_O"], dtype=f32),
        "gate_W": np.ascontiguousarray(inputs["L0_ffn_gate_W"], dtype=f32),
        "W1": np.ascontiguousarray(inputs["L0_ffn_W1"], dtype=f32),
        "b1": np.ascontiguousarray(inputs["L0_ffn_b1"], dtype=f32),
        "W2": np.ascontiguousarray(inputs["L0_ffn_W2"], dtype=f32),
        "b2": np.ascontiguousarray(inputs["L0_ffn_b2"], dtype=f32),
        "Ws1": np.ascontiguousarray(inputs["L0_ffn_Ws1"], dtype=f32),
        "bs1": np.ascontiguousarray(inputs["L0_ffn_bs1"], dtype=f32),
        "Ws2": np.ascontiguousarray(inputs["L0_ffn_Ws2"], dtype=f32),
        "bs2": np.ascontiguousarray(inputs["L0_ffn_bs2"], dtype=f32),
        "alpha_bc": alpha_bc,
        "rs_bc": rs_bc,
        "identf": np.eye(128, dtype=f32),
        "identb": np.eye(128, dtype=ml_dtypes.bfloat16),
        "ones_bf": np.ones((1, 128), dtype=ml_dtypes.bfloat16),
    }
    nc = build_nc(T=T)
    in_maps = [dict(common, x=np.ascontiguousarray(x[b])) for b in range(B)]
    res = run_bass_kernel_spmd(nc, in_maps, core_ids=list(range(B)))
    out = np.stack([res.results[b]["out"] for b in range(B)], axis=0)
    return out


# version 32 (final)
# speedup vs baseline: 1.0647x; 1.0647x over previous
import numpy as np
import ml_dtypes
import concourse.bass as bass
import concourse.mybir as mybir

F32 = mybir.dt.float32
BF16 = mybir.dt.bfloat16
AF = mybir.ActivationFunctionType
ALU = mybir.AluOpType
AX = mybir.AxisListType

D = 1024
NK = 8
H = 512
NHC = 4
E = 8
HEADS = 16
NPAIR = 8

ENGS = ("PE", "ACT", "DVE", "POOL", "SP")


class Plan:
    NDMA = 4

    def __init__(self):
        self.items = {e: [] for e in ENGS}
        self.cnt = {}
        self.waited = {}
        self.dmai = {e: 0 for e in ENGS}

    def op(self, eng, fn, deps=(), sig=True, dma=False):
        if dma:
            sem = "d" + eng + str(self.dmai[eng] % self.NDMA)
            self.dmai[eng] += 1
            step = 16
            deps = list(deps) + [(sem, self.cnt.get(sem, 0))]
        else:
            sem = "c" + eng
            step = 1
        waits = []
        for d in deps:
            if d is None:
                continue
            s, c = d
            if c <= 0:
                continue
            k = (eng, s)
            if self.waited.get(k, 0) < c:
                waits.append((s, c))
                self.waited[k] = c
        inc = None
        tok = None
        if sig:
            self.cnt[sem] = self.cnt.get(sem, 0) + step
            inc = (sem, step)
            tok = (sem, self.cnt[sem])
        self.items[eng].append((fn, waits, inc))
        return tok

    def qbar(self, eng):
        out = []
        for i in range(self.NDMA):
            sem = f"d{eng}{i}"
            c = self.cnt.get(sem, 0)
            if c > 0:
                out.append((sem, c))
        return out

    def dma(self, eng, dst, src, deps=(), **kw):
        return self.op(eng, lambda e, dst=dst, src=src, kw=kw: e.dma_start(out=dst, in_=src, **kw),
                       deps=deps, dma=True)

    def emit(self, eng_name, eng, sems):
        for fn, waits, inc in self.items[eng_name]:
            for s, c in waits:
                eng.wait_ge(sems[s], c)
            ins = fn(eng)
            if inc is not None:
                ins.then_inc(sems[inc[0]], inc[1])


class Res:
    __slots__ = ("w", "rs")
    def __init__(self):
        self.w = None
        self.rs = []
    def rdep(self):
        return [self.w] if self.w else []
    def wdep(self):
        d = list(self.rs)
        if self.w:
            d.append(self.w)
        return d
    def note_r(self, tok):
        if tok:
            self.rs.append(tok)
    def note_w(self, tok):
        if tok:
            self.w = tok
            self.rs = []


def build_nc(T=2048, dbg=False):
    assert T % 512 == 0
    NT = T // 128
    SW = 512
    NS = T // SW
    NST = SW // 128
    NC = NT

    nc = bass.Bass("TRN2", target_bir_lowering=False, debug=False)

    x_d = nc.dram_tensor("x", [T, D], F32, kind="ExternalInput").ap()
    out_d = nc.dram_tensor("out", [T, D], F32, kind="ExternalOutput").ap()
    watt_d = [nc.dram_tensor(n, [D, D], F32, kind="ExternalInput").ap()
              for n in ("W_Q", "W_K", "W_V", "W_K_prime", "W_O")]
    gw_d = nc.dram_tensor("gate_W", [D, E], F32, kind="ExternalInput").ap()
    w1_d = nc.dram_tensor("W1", [E, D, H], F32, kind="ExternalInput").ap()
    b1_d = nc.dram_tensor("b1", [E, H], F32, kind="ExternalInput").ap()
    w2_d = nc.dram_tensor("W2", [E, H, D], F32, kind="ExternalInput").ap()
    b2_d = nc.dram_tensor("b2", [E, D], F32, kind="ExternalInput").ap()
    ws1_d = nc.dram_tensor("Ws1", [D, H], F32, kind="ExternalInput").ap()
    bs1_d = nc.dram_tensor("bs1", [H], F32, kind="ExternalInput").ap()
    ws2_d = nc.dram_tensor("Ws2", [H, D], F32, kind="ExternalInput").ap()
    bs2_d = nc.dram_tensor("bs2", [D], F32, kind="ExternalInput").ap()
    alpha_d = nc.dram_tensor("alpha_bc", [128, 1], F32, kind="ExternalInput").ap()
    rsc_d = nc.dram_tensor("rs_bc", [128, 1], F32, kind="ExternalInput").ap()
    identf_d = nc.dram_tensor("identf", [128, 128], F32, kind="ExternalInput").ap()
    identb_d = nc.dram_tensor("identb", [128, 128], BF16, kind="ExternalInput").ap()
    onesb_d = nc.dram_tensor("ones_bf", [1, 128], BF16, kind="ExternalInput").ap()
    xpark_d = nc.dram_tensor("xpark", [T, D], F32, kind="Internal").ap()
    w1bf_d = nc.dram_tensor("w1bf", [2, 128, NK * H], BF16, kind="Internal").ap()
    w2bf_d = nc.dram_tensor("w2bf", [2, 128, NHC * D], BF16, kind="Internal").ap()
    wtbf_d = nc.dram_tensor("wtbf", [4, NK, 128, D], BF16, kind="Internal").ap()
    wotbf_d = nc.dram_tensor("wotbf", [NK, 128, D], BF16, kind="Internal").ap()

    A_f = nc.alloc_sbuf_tensor("arenaA", [128, NT * D], F32)
    A_b = A_f.bitcast(BF16)
    B_b = nc.alloc_sbuf_tensor("arenaB", [128, NK * T], BF16)
    C_cols = max(NK * T, HEADS * NC * 66)
    C_b = nc.alloc_sbuf_tensor("arenaC", [128, C_cols], BF16)
    D_b = nc.alloc_sbuf_tensor("arenaD", [128, max(8 * T, 16384)], BF16)
    E_f = nc.alloc_sbuf_tensor("arenaE", [128, 4608], F32)
    E_b = E_f.bitcast(BF16)

    gelu_s = [nc.alloc_sbuf_tensor(f"gelu{i}", [128, SW], F32) for i in range(2)]
    xn_s = [nc.alloc_sbuf_tensor(f"xnb{i}", [128, D], BF16) for i in range(2)]
    stat_s = [nc.alloc_sbuf_tensor(f"stat{i}", [128, 16], F32) for i in range(3)]

    identf_s = nc.alloc_sbuf_tensor("identf_s", [128, 128], F32)
    identb_s = nc.alloc_sbuf_tensor("identb_s", [128, 128], BF16)
    onesb_s = nc.alloc_sbuf_tensor("onesb_s", [1, 128], BF16)
    alpha_s = nc.alloc_sbuf_tensor("alpha_s", [128, 1], F32)
    oma_s = nc.alloc_sbuf_tensor("oma_s", [128, 1], F32)
    rsc_s = nc.alloc_sbuf_tensor("rsc_s", [128, 1], F32)
    eps_s = nc.alloc_sbuf_tensor("eps_s", [128, 1], F32)
    one_s = nc.alloc_sbuf_tensor("one_s", [128, 1], F32)
    b1_s = nc.alloc_sbuf_tensor("b1_s", [128, 40], F32)
    b2row_s = nc.alloc_sbuf_tensor("b2row_s", [E, D], BF16)
    bs2row_s = nc.alloc_sbuf_tensor("bs2row_s", [1, D], BF16)
    gateW_s = nc.alloc_sbuf_tensor("gateW_s", [128, NK * E], BF16)
    gfm_s = nc.alloc_sbuf_tensor("gfm_s", [E, SW], F32)
    gfin_s = nc.alloc_sbuf_tensor("gfin_s", [E, T], BF16)
    g_s = nc.alloc_sbuf_tensor("g_s", [128, NT * E], F32)
    gsc_s = nc.alloc_sbuf_tensor("gsc_s", [128, 32], F32)
    s2row_s = nc.alloc_sbuf_tensor("s2row_s", [1, SW], F32)
    rdrow_s = nc.alloc_sbuf_tensor("rdrow_s", [1, SW], F32)
    rsinv_s = nc.alloc_sbuf_tensor("rsinv_s", [1, 2 * SW], F32)
    mk_s = nc.alloc_sbuf_tensor("mk_s", [1, 256], BF16)

    PS = [nc.alloc_psum_tensor(f"ps{i}", [128, 1024], F32) for i in range(4)]
    PSH = []
    for i in range(4):
        PSH.append(PS[i][:, 0:512])
        PSH.append(PS[i][:, 512:1024])
    PS_bf = [p.bitcast(BF16) for p in PS]

    P = Plan()

    r_bank = [Res() for _ in range(8)]
    r_x = [Res() for _ in range(NT * 2)]
    r_xnT = Res()
    r_xn = [Res(), Res()]
    r_stat = [Res() for _ in range(3)]
    r_gelu = [Res() for _ in range(2)]
    r_w1 = [Res() for _ in range(2)]
    r_w2 = [Res() for _ in range(2)]
    r_hg = [Res() for _ in range(2)]
    r_stage = [Res() for _ in range(6)]
    r_wrow = [Res() for _ in range(3)]
    r_wts = [Res() for _ in range(2)]
    r_gfm = Res()
    r_gfin = Res()
    r_grow = [Res() for _ in range(2)]
    r_g = Res()
    r_gsc = [Res() for _ in range(2)]
    r_s2row = Res()
    r_rdrow = Res()
    r_gmk = Res()
    r_A = [Res() for _ in range(4)]
    r_oun = [Res() for _ in range(2 * NS)]
    r_rsinv = [Res(), Res()]
    r_kD = Res()
    r_vfm = [Res() for _ in range(2)]
    r_vones = Res()
    r_vaug = Res()
    r_wt = [Res() for _ in range(4)]
    r_park = [Res() for _ in range(NT)]
    r_t12 = Res()
    r_scr = Res()

    x_t = lambda tt: A_f[:, tt * D:(tt + 1) * D]
    x_h = lambda tt, dc: A_f[:, tt * D + dc * 512: tt * D + (dc + 1) * 512]
    q_ch = lambda m: A_b[:, m * T:(m + 1) * T]
    kp_ch = lambda m: A_b[:, NK * T + m * T: NK * T + (m + 1) * T]
    xnT_ch = lambda k: B_b[:, k * T:(k + 1) * T]
    omix_ch = lambda m: B_b[:, m * T:(m + 1) * T]
    k_ch = lambda m: D_b[:, m * T:(m + 1) * T]
    wo_ch = lambda dc: D_b[:, 8192 + dc * D: 8192 + (dc + 1) * D]
    w1v = lambda buf: D_b[:, buf * 4096:(buf + 1) * 4096]
    w2v = lambda buf: D_b[:, 8192 + buf * 4096: 8192 + (buf + 1) * 4096]
    hg_v = lambda buf, hc: C_b[:, (buf * 4 + hc) * T:(buf * 4 + hc + 1) * T]
    vaug = lambda h, c: C_b[:, (h * NC + c) * 66:(h * NC + c) * 66 + 65]
    stg = lambda s: (E_f[:, s * 512:(s + 1) * 512] if s < 4 else gelu_s[s - 4][:])
    NSTG = 6
    _stg_i = [0]
    def stg_slot():
        s = _stg_i[0] % NSTG
        _stg_i[0] += 1
        return s
    SQ = ("SP", "ACT", "POOL")
    _q_i = [0]
    def nxq():
        q = SQ[_q_i[0] % 3]
        _q_i[0] += 1
        return q
    wrow_v = lambda buf: (E_b[:, 4096 + buf * 1024: 4096 + (buf + 1) * 1024] if buf < 2
                          else E_b[:, 8192:9216])
    wts_v = lambda buf: E_b[:, 6144 + buf * 1024: 6144 + (buf + 1) * 1024]
    vfm_v = lambda buf: E_b[:, buf * 2048:(buf + 1) * 2048]
    gelu_b = [g.bitcast(BF16) for g in gelu_s]
    A_v = lambda buf: (E_b[:, buf * 1024:(buf + 1) * 1024] if buf < 2 else gelu_b[buf - 2][:, 0:1024])
    oun_v = lambda br, s: E_b[:, 2048 + (br * NS + s) * SW: 2048 + (br * NS + s + 1) * SW]
    gmk_v = E_f[0:E, 4096:4096 + SW]
    t2_s = E_f[:, 3584:4096]
    scr_row = E_f[0:1, 3072:3584]
    rsb_bufs = [
        [rsinv_s.bitcast(BF16)[0:1, i * 512:(i + 1) * 512] for i in range(4)],
        [E_b[0:1, 6144:6656], E_b[0:1, 6656:7168],
         E_b[0:1, 8192:8704], E_b[0:1, 8704:9216]],
    ]
    grow_v = E_b[0:1, 8192:8192 + 2 * SW]
    wtproj_v = lambda buf: E_b[:, 4096 + buf * 1024: 4096 + (buf + 1) * 1024]

    def regs_of(c0_bf, c1_bf):
        f0, f1 = c0_bf // 2, (c1_bf + 1) // 2
        return list(range(f0 // 512, min((f1 + 511) // 512, NT * 2)))

    dbg_toks = {}

    def dump(name, src_ap, shape, dtype, res_list=()):
        if not dbg or (isinstance(dbg, (set, frozenset, list, tuple)) and name not in dbg):
            return
        dd = nc.dram_tensor("dbg_" + name, shape, dtype, kind="ExternalOutput").ap()
        bar = [(cs, P.cnt.get(cs, 0)) for cs in ("cPE", "cACT", "cDVE", "cPOOL")]
        for q in ("SP", "POOL", "ACT"):
            bar += P.qbar(q)
        tok = P.dma("POOL", dd, src_ap, deps=[b for b in bar if b[1] > 0])
        for r in res_list:
            r.note_r(tok)
        dbg_toks[name] = tok

    t_idf = P.dma("SP", identf_s[:], identf_d)
    t_idb = P.dma("SP", identb_s[:], identb_d)
    t_onb = P.dma("SP", onesb_s[:], onesb_d)
    t_al = P.dma("SP", alpha_s[:], alpha_d)
    t_rs = P.dma("SP", rsc_s[:], rsc_d)
    t_ep = P.op("DVE", lambda e: e.memset(eps_s[:], 1e-5))
    t_on = P.op("DVE", lambda e: e.memset(one_s[:], 1.0))
    t_oma = P.op("DVE", lambda e: e.tensor_tensor(out=oma_s[:], in0=one_s[:], in1=alpha_s[:],
                                                  op=ALU.subtract), deps=[t_al, t_on])
    P.op("DVE", lambda e: e.memset(mk_s[0:1, 0:64], 1.0))
    P.op("DVE", lambda e: e.memset(mk_s[0:1, 64:192], 0.0))
    t_mk = P.op("DVE", lambda e: e.memset(mk_s[0:1, 192:256], 1.0))
    t_b1f = P.dma("SP", E_f[0:8, 3072:3584], b1_d)
    t_bs1f = P.dma("SP", E_f[8:9, 3072:3584], bs1_d.rearrange("(a b) -> a b", a=1))
    r_wts[0].note_w(t_bs1f)
    tb_all = []
    for hc in range(NHC):
        tt_tp = P.op("PE", lambda e, hc=hc: e.matmul(
            PS[0][:, hc * 16:hc * 16 + 9], E_f[0:9, 3072 + hc * 128:3072 + (hc + 1) * 128],
            identf_s[0:9, 0:9], is_transpose=True, start=True, stop=True),
            deps=[t_b1f, t_bs1f, t_idf] + (r_bank[0].wdep() if hc == 0 else []))
        r_bank[0].note_w(tt_tp)
        r_wts[0].note_r(tt_tp)
        t_c1 = P.op("DVE", lambda e, hc=hc: e.tensor_copy(
            out=b1_s[:, hc * 9:hc * 9 + 9],
            in_=PS[0][:, hc * 16:hc * 16 + 9]), deps=[tt_tp])
        r_bank[0].note_r(t_c1)
        tb_all.append(t_c1)
    t_b2f = P.dma("SP", E_f[0:E, 0:D], b2_d)
    t_b2 = P.op("DVE", lambda e: e.tensor_copy(out=b2row_s[:], in_=E_f[0:E, 0:D]), deps=[t_b2f])
    t_bs2f = P.dma("SP", E_f[0:1, D:2 * D], bs2_d.rearrange("(a b) -> a b", a=1))
    t_bs2 = P.op("DVE", lambda e: e.tensor_scalar(out=bs2row_s[:], in0=E_f[0:1, D:2 * D],
                                                  scalar1=0.5, scalar2=None, op0=ALU.mult),
                 deps=[t_bs2f])
    t_gwf = P.dma("SP", E_f[:, 2048:2048 + NK * E].rearrange("p (k e) -> p k e", k=NK),
                  gw_d.rearrange("(k p) e -> p k e", p=128))
    t_gw = P.op("DVE", lambda e: e.tensor_copy(out=gateW_s[:], in_=E_f[:, 2048:2048 + NK * E]),
                deps=[t_gwf])
    r_stage[0].note_w(t_b2f)
    r_stage[0].note_r(t_b2)
    r_stage[1].note_w(t_b2f)
    r_stage[1].note_r(t_b2)
    r_stage[2].note_w(t_bs2f)
    r_stage[2].note_r(t_bs2)
    r_stage[3].note_w(t_bs2f)
    r_stage[3].note_r(t_bs2)
    r_wrow[0].note_w(t_gwf)
    r_wrow[0].note_r(t_gw)

    for tt in range(NT):
        tok = P.dma(SQ[tt % 3], x_t(tt), x_d[tt * 128:(tt + 1) * 128, :])
        r_x[2 * tt].note_w(tok)
        r_x[2 * tt + 1].note_w(tok)

    xnT_cur = {}

    def emit_ln(park=False, tile_war=None):
        war = r_xnT.wdep() if tile_war is None else []
        t7 = None
        xnT_cur.clear()
        for tt in range(NT):
            sb = tt % 3
            st = stat_s[sb]
            dep_x = r_x[2 * tt].rdep() + r_x[2 * tt + 1].rdep()
            t1 = P.op("DVE", lambda e, tt=tt, st=st: e.bn_stats(out=st[:, 0:6], in_=A_f[:, tt * D: tt * D + 512]),
                      deps=dep_x + r_stat[sb].wdep())
            t2 = P.op("DVE", lambda e, tt=tt, st=st: e.bn_stats(out=st[:, 6:12], in_=A_f[:, tt * D + 512: tt * D + 1024]),
                      deps=dep_x)
            t3 = P.op("DVE", lambda e, st=st: e.bn_aggr(out=st[:, 12:14], in_=st[:, 0:12]),
                      deps=[t1, t2])
            t4 = P.op("ACT", lambda e, st=st: e.activation(out=st[:, 14:15], in_=st[:, 13:14],
                                                           func=AF.Sqrt, bias=eps_s[:]),
                      deps=[t3, t_ep])
            t5 = P.op("DVE", lambda e, st=st: e.reciprocal(out=st[:, 15:16], in_=st[:, 14:15]),
                      deps=[t4])
            xp = tt % 2
            xb = xn_s[xp]
            t6 = P.op("DVE", lambda e, tt=tt, st=st, xb=xb: e.tensor_scalar(
                out=xb[:], in0=A_f[:, tt * D:(tt + 1) * D], scalar1=st[:, 12:13],
                scalar2=st[:, 15:16], op0=ALU.subtract, op1=ALU.mult),
                deps=[t5] + r_xn[xp].wdep())
            r_xn[xp].note_w(t6)
            r_stat[sb].note_w(t5)
            r_stat[sb].note_r(t6)
            r_x[2 * tt].note_r(t6)
            r_x[2 * tt + 1].note_r(t6)
            if park:
                tokp = P.dma("POOL", xpark_d[tt * 128:(tt + 1) * 128, :], x_t(tt), deps=[t6])
                r_x[2 * tt].note_r(tokp)
                r_x[2 * tt + 1].note_r(tokp)
                r_park[tt].note_w(tokp)
            lnb = 7 - (tt % 2)
            bank = r_bank[lnb]
            psv = PS_bf[3][:, 1024:2048] if lnb == 7 else PS_bf[3][:, 0:1024]
            mm_dep = [t6, t_idb] + bank.wdep()
            tokm = None
            for k in range(NK):
                tokm = P.op("PE", lambda e, k=k, xb=xb, psv=psv: e.matmul(
                    psv[:, k * 128:(k + 1) * 128],
                    xb[:, k * 128:(k + 1) * 128], identb_s[:],
                    is_transpose=True, start=(k == 0), stop=(k == NK - 1)),
                    deps=(mm_dep if k == 0 else []), sig=(k == NK - 1))
            r_xn[xp].note_r(tokm)
            bank.note_w(tokm)
            dst = B_b.rearrange("p (k t j) -> p t k j", k=NK, t=NT, j=128)[:, tt]
            t7 = P.op("DVE", lambda e, dst=dst, psv=psv: e.tensor_copy(out=dst, in_=psv),
                      deps=[tokm] + (war if tile_war is None else [tile_war[tt]]))
            bank.note_r(t7)
            xnT_cur[tt] = t7
        r_xnT.note_w(t7)

    gtoks = {}
    gftoks = {}
    gf_defer = []

    def emit_gate():
        t_gf = None
        for s in range(NS):
            b6 = r_bank[6]
            deps = b6.wdep() + [xnT_cur[s * NST + NST - 1], t_gw]
            tokm = None
            for k in range(NK):
                tokm = P.op("PE", lambda e, k=k, s=s: e.matmul(
                    PSH[6][0:E, 0:SW], gateW_s[:, k * E:(k + 1) * E],
                    xnT_ch(k)[:, s * SW:(s + 1) * SW], start=(k == 0), stop=(k == NK - 1)),
                    deps=(deps if k == 0 else []), sig=(k == NK - 1))
            r_xnT.note_r(tokm)
            b6.note_w(tokm)
            t_sig = P.op("ACT", lambda e: e.activation(out=gfm_s[:], in_=PSH[6][0:E, 0:SW],
                                                       func=AF.Sigmoid),
                         deps=[tokm] + r_gfm.wdep())
            b6.note_r(t_sig)
            t_grs = P.op("DVE", lambda e: e.tensor_scalar(out=gfm_s[:], in0=gfm_s[:],
                                                          scalar1=rsc_s[0:E, :], scalar2=None,
                                                          op0=ALU.mult),
                         deps=[t_sig, t_rs])
            r_gfm.note_w(t_grs)
            for j in range(NST):
                tt = s * NST + j
                p2 = tt % 2
                b7 = r_bank[7]
                tok_tp = P.op("PE", lambda e, j=j: e.matmul(
                    PS[3][:, 512:520], gfm_s[0:E, j * 128:(j + 1) * 128], identf_s[0:E, 0:E],
                    is_transpose=True, start=True, stop=True),
                    deps=[t_grs, t_idf] + b7.wdep())
                b7.note_w(tok_tp)
                r_gfm.note_r(tok_tp)
                t_ev = P.op("DVE", lambda e, tt=tt: e.tensor_copy(
                    out=g_s[:, tt * E:(tt + 1) * E], in_=PS[3][:, 512:520]),
                    deps=[tok_tp] + (r_g.wdep() if tt == 0 else []))
                b7.note_r(t_ev)
                gv = g_s[:, tt * E:(tt + 1) * E]
                sc = gsc_s[:, p2 * 16:p2 * 16 + 16]
                tA = P.op("DVE", lambda e, gv=gv, sc=sc: e.reduce_max(out=sc[:, 8:9], in_=gv, axis=AX.X),
                          deps=[t_ev] + r_gsc[p2].wdep())
                tB = P.op("DVE", lambda e, gv=gv, sc=sc: e.tensor_scalar(
                    out=sc[:, 0:8], in0=gv, scalar1=sc[:, 8:9], scalar2=None, op0=ALU.is_ge),
                    deps=[tA])
                tC = P.op("DVE", lambda e, gv=gv, sc=sc: e.scalar_tensor_tensor(
                    out=sc[:, 0:8], in0=sc[:, 0:8], scalar=-1e30, in1=gv,
                    op0=ALU.mult, op1=ALU.add), deps=[tB])
                tD = P.op("DVE", lambda e, sc=sc: e.reduce_max(out=sc[:, 9:10], in_=sc[:, 0:8], axis=AX.X),
                          deps=[tC])
                tE = P.op("DVE", lambda e, sc=sc: e.tensor_tensor(
                    out=sc[:, 10:11], in0=sc[:, 8:9], in1=sc[:, 9:10], op=ALU.add), deps=[tD])
                tF = P.op("DVE", lambda e, sc=sc: e.reciprocal(out=sc[:, 11:12], in_=sc[:, 10:11]),
                          deps=[tE])
                tG = P.op("DVE", lambda e, sc=sc: e.tensor_scalar(
                    out=sc[:, 12:13], in0=sc[:, 11:12], scalar1=0.5, scalar2=None, op0=ALU.mult),
                    deps=[tF])
                tH = P.op("DVE", lambda e, gv=gv, sc=sc: e.tensor_scalar(
                    out=sc[:, 0:8], in0=gv, scalar1=sc[:, 9:10], scalar2=None, op0=ALU.is_ge),
                    deps=[tD, tC])
                tI = P.op("DVE", lambda e, gv=gv, sc=sc: e.scalar_tensor_tensor(
                    out=gv, in0=sc[:, 0:8], scalar=sc[:, 12:13], in1=gv,
                    op0=ALU.mult, op1=ALU.mult), deps=[tG, tH])
                r_gsc[p2].note_w(tI)
                gtoks[tt] = tI

                def mk_gf(tt=tt, gv=gv, tI=tI):
                    b7b = r_bank[7]
                    tok2 = P.op("PE", lambda e, gv=gv: e.matmul(
                        PS[3][0:E, 640:768], gv, identf_s[:],
                        is_transpose=True, start=True, stop=True),
                        deps=[tI, t_idf] + b7b.wdep())
                    b7b.note_w(tok2)
                    r_g.note_r(tok2)
                    t_gf = P.op("ACT", lambda e, tt=tt: e.activation(
                        out=gfin_s[:, tt * 128:(tt + 1) * 128], in_=PS[3][0:E, 640:768],
                        func=AF.Copy), deps=[tok2] + (r_gfin.wdep() if tt == 0 else []))
                    b7b.note_r(t_gf)
                    gftoks[tt] = t_gf
                    return t_gf
                gf_defer.append(mk_gf)

    park_toks = {}

    def emit_ffn(prep=None, park01=False, pre=None):
        def stream_expert(e_i, buf):
            d1 = r_w1[buf].wdep()
            d2 = r_w2[buf].wdep()
            w1src = w1_d[e_i].rearrange("(k p) h -> p k h", p=128) if e_i < E else \
                    ws1_d.rearrange("(k p) h -> p k h", p=128)
            w2src = w2_d[e_i].rearrange("(k p) h -> p k h", p=128) if e_i < E else \
                    ws2_d.rearrange("(k p) h -> p k h", p=128)
            t_w1 = t_w2 = None
            for i in range(NK):
                sb = stg_slot()
                tl = P.dma(SQ[i % 3], stg(sb), w1src[:, i, :], deps=r_stage[sb].wdep())
                r_stage[sb].note_w(tl)
                t_w1 = P.op("DVE", lambda e, i=i, buf=buf, sb=sb: e.tensor_copy(
                    out=w1v(buf)[:, i * 512:(i + 1) * 512], in_=stg(sb)),
                    deps=[tl] + (d1 if i == 0 else []))
                r_stage[sb].note_r(t_w1)
            for i in range(NK):
                sb = stg_slot()
                hc, j = i // 2, i % 2
                tl = P.dma(SQ[(i + 1) % 3], stg(sb), w2src[:, hc, j * 512:(j + 1) * 512],
                           deps=r_stage[sb].wdep())
                r_stage[sb].note_w(tl)
                if e_i < E:
                    t_w2 = P.op("DVE", lambda e, buf=buf, sb=sb, hc=hc, j=j: e.tensor_copy(
                        out=w2v(buf)[:, hc * D + j * 512: hc * D + (j + 1) * 512], in_=stg(sb)),
                        deps=[tl] + (d2 if i == 0 else []))
                else:
                    t_w2 = P.op("DVE", lambda e, buf=buf, sb=sb, hc=hc, j=j: e.tensor_scalar(
                        out=w2v(buf)[:, hc * D + j * 512: hc * D + (j + 1) * 512], in0=stg(sb),
                        scalar1=0.5, scalar2=None, op0=ALU.mult),
                        deps=[tl] + (d2 if i == 0 else []))
                r_stage[sb].note_r(t_w2)
            r_w1[buf].note_w(t_w1)
            r_w2[buf].note_w(t_w2)
            if park01 and e_i < 2:
                ts1 = P.dma(nxq(), w1bf_d[e_i], w1v(buf), deps=[t_w1])
                ts2 = P.dma(nxq(), w2bf_d[e_i], w2v(buf), deps=[t_w2])
                r_w1[buf].note_r(ts1)
                r_w2[buf].note_r(ts2)
                park_toks[e_i] = (ts1, ts2)
            return t_w1, t_w2

        def get_toks(e_i, buf):
            if pre is not None and e_i in pre:
                return pre[e_i], pre_w2[e_i]
            return stream_expert(e_i, buf)

        def pstep():
            if prep is not None:
                prep.step()

        w_toks = {0: get_toks(0, 0)}
        for e_i in range(E + 1):
            buf = e_i % 2
            t_w1, t_w2 = w_toks.pop(e_i)
            if e_i < E:
                w_toks[e_i + 1] = get_toks(e_i + 1, (e_i + 1) % 2)
            if e_i == 1 and gf_defer:
                t_gf_last = None
                for f in gf_defer:
                    t_gf_last = f()
                gf_defer.clear()
                r_gfin.note_w(t_gf_last)
            hg_war = r_hg[buf].wdep()
            slice_art = {}
            def emit_mm1_slice(s):
                hg_toks = {}
                def mm_group(hc):
                    hb = 4 + (s * NHC + hc) % 2
                    bank = r_bank[hb]
                    deps = bank.wdep() + [t_w1] + ([xnT_cur[s * NST + NST - 1]] if hc == 0 else [])
                    tokm = None
                    for k in range(NK):
                        tokm = P.op("PE", lambda e, k=k, hc=hc, s=s, buf=buf, hb=hb: e.matmul(
                            PSH[hb][:, 0:SW], w1v(buf)[:, k * H + hc * 128: k * H + (hc + 1) * 128],
                            xnT_ch(k)[:, s * SW:(s + 1) * SW], start=(k == 0), stop=(k == NK - 1)),
                            deps=(deps if k == 0 else []), sig=(k == NK - 1))
                    r_xnT.note_r(tokm)
                    r_w1[buf].note_r(tokm)
                    bank.note_w(tokm)
                    bcol = hc * 9 + (e_i if e_i < E else 8)
                    t_hg = P.op("ACT", lambda e, hc=hc, s=s, buf=buf, hb=hb, bcol=bcol: e.activation(
                        out=hg_v(buf, hc)[:, s * SW:(s + 1) * SW], in_=PSH[hb][:, 0:SW],
                        func=AF.Gelu, bias=b1_s[:, bcol: bcol + 1]),
                        deps=[tokm] + tb_all + (hg_war if (s == 0 and hc == 0) else []))
                    bank.note_r(t_hg)
                    hg_toks[hc] = t_hg
                for hc in range(NHC):
                    mm_group(hc)
                slice_art[s] = hg_toks

            def emit_mm2_slice(s):
                hg_toks = slice_art[s]
                for j in range(NST):
                    tt = s * NST + j
                    for dc in range(2):
                        yb = (tt * 2 + dc) % 4
                        bank = r_bank[yb]
                        n_mm = NHC + (2 if e_i == E else 0)
                        deps = bank.wdep() + [hg_toks[h2] for h2 in range(NHC)] + [t_w2]
                        im = 0
                        tokm = None
                        for hc in range(NHC):
                            im += 1
                            tokm = P.op("PE", lambda e, hc=hc, tt=tt, dc=dc, buf=buf, yb=yb, im=im, n_mm=n_mm: e.matmul(
                                PSH[yb][:, 0:512], hg_v(buf, hc)[:, tt * 128:(tt + 1) * 128],
                                w2v(buf)[:, hc * D + dc * 512: hc * D + (dc + 1) * 512],
                                start=(im == 1), stop=(im == n_mm)),
                                deps=(deps if im == 1 else []), sig=(im == n_mm))
                        if e_i == E:
                            P.op("PE", lambda e, tt=tt, dc=dc, yb=yb: e.matmul(
                                PSH[yb][:, 0:512], gfin_s[:, tt * 128:(tt + 1) * 128],
                                b2row_s[0:E, dc * 512:(dc + 1) * 512], start=False, stop=False),
                                deps=[t_b2, gftoks[tt]] + r_gfin.rdep(), sig=False)
                            tokm = P.op("PE", lambda e, tt=tt, dc=dc, yb=yb: e.matmul(
                                PSH[yb][:, 0:512], onesb_s[:],
                                bs2row_s[0:1, dc * 512:(dc + 1) * 512], start=False, stop=True),
                                deps=[t_bs2, t_onb], sig=True)
                            r_gfin.note_r(tokm)
                        r_hg[buf].note_r(tokm)
                        r_w2[buf].note_r(tokm)
                        bank.note_w(tokm)
                        reg = r_x[2 * tt + dc]
                        if e_i < E:
                            t_add = P.op("DVE", lambda e, tt=tt, dc=dc, yb=yb, e_i=e_i: e.scalar_tensor_tensor(
                                out=x_h(tt, dc), in0=PSH[yb][:, 0:512],
                                scalar=g_s[:, tt * E + e_i: tt * E + e_i + 1],
                                in1=x_h(tt, dc), op0=ALU.mult, op1=ALU.add),
                                deps=[tokm, gtoks[tt]] + reg.wdep())
                            r_g.note_r(t_add)
                        else:
                            t_add = P.op("DVE", lambda e, tt=tt, dc=dc, yb=yb: e.tensor_tensor(
                                out=x_h(tt, dc), in0=x_h(tt, dc), in1=PSH[yb][:, 0:512], op=ALU.add),
                                deps=[tokm] + reg.wdep())
                        bank.note_r(t_add)
                        reg.note_w(t_add)

            emit_mm1_slice(0)
            pstep()
            for s in range(1, NS):
                emit_mm1_slice(s)
                pstep()
                emit_mm2_slice(s - 1)
                pstep()
            emit_mm2_slice(NS - 1)
            pstep()


    class PrepSched:
        def __init__(self, jobs):
            self.jobs = list(jobs)
            self.li = 0
            self.ti = 0
            self.art = {}

        def _load(self, i):
            w, m = self.jobs[i]
            wb = i % 3
            tcs = []
            for h2 in range(2):
                sb = stg_slot()
                tl = P.dma(nxq(), stg(sb), watt_d[w][m * 128:(m + 1) * 128, h2 * 512:(h2 + 1) * 512],
                           deps=r_stage[sb].wdep())
                r_stage[sb].note_w(tl)
                tc = P.op("DVE", lambda e, sb=sb, wb=wb, h2=h2: e.tensor_copy(
                    out=wrow_v(wb)[:, h2 * 512:(h2 + 1) * 512], in_=stg(sb)),
                    deps=[tl] + (r_wrow[wb].wdep() if h2 == 0 else []))
                r_stage[sb].note_r(tc)
                tcs.append(tc)
            r_wrow[wb].note_w(tcs[-1])
            self.art[i] = tcs

        def _tp(self, i):
            w, m = self.jobs[i]
            wb = i % 3
            wsb = i % 2
            tcs = self.art.pop(i)
            tokt = None
            for k in range(NK):
                tokt = P.op("PE", lambda e, k=k, wb=wb: e.matmul(
                    PS_bf[3][:, k * 128:(k + 1) * 128], wrow_v(wb)[:, k * 128:(k + 1) * 128],
                    identb_s[:], is_transpose=True, start=True, stop=True),
                    deps=[tcs[k // 4], t_idb] + (r_bank[6].wdep() if k == 0 else []))
            r_bank[6].note_w(tokt)
            r_wrow[wb].note_r(tokt)
            t_ev = P.op("DVE", lambda e, wsb=wsb: e.tensor_copy(
                out=wts_v(wsb)[:, 0:D], in_=PS_bf[3][:, 0:1024]),
                deps=[tokt] + r_wts[wsb].wdep())
            r_bank[6].note_r(t_ev)
            r_wts[wsb].note_w(t_ev)
            if w < 4:
                ts = P.dma(nxq(), wtbf_d[w, m], wts_v(wsb)[:, 0:D], deps=[t_ev])
                r_wts[wsb].note_r(ts)
            else:
                for k in range(NK):
                    ts = P.dma(nxq(), wotbf_d[k][:, m * 128:(m + 1) * 128],
                               wts_v(wsb)[:, k * 128:(k + 1) * 128], deps=[t_ev])
                    r_wts[wsb].note_r(ts)

        def step(self):
            if self.ti < len(self.jobs) and self.ti <= self.li - 2:
                self._tp(self.ti)
                self.ti += 1
            if self.li < len(self.jobs):
                self._load(self.li)
                self.li += 1

        def flush(self):
            while self.ti < len(self.jobs):
                if self.li < len(self.jobs):
                    self._load(self.li)
                    self.li += 1
                self._tp(self.ti)
                self.ti += 1

    emit_ln(park=False)
    dump("xnT1", B_b[:], [128, NK * T], BF16, [r_xnT])
    dump("b1s", b1_s[:], [128, 40], F32, [])
    emit_gate()
    dump("gfin1", gfin_s[:], [E, T], BF16, [r_gfin])
    prep = PrepSched([(w, m) for w in range(5) for m in range(NK)])
    emit_ffn(prep=prep, park01=True)
    prep.flush()
    dump("x1", A_f[:], [128, NT * D], F32, [r_x[i] for i in range(2 * NT)])

    emit_ln(park=True)

    t_vones = P.op("DVE", lambda e: e.memset(C_b[:, 64:HEADS * NC * 66:66], 1.0),
                   deps=r_hg[0].wdep() + r_hg[1].wdep() + r_vones.wdep())
    t_vpad = P.op("DVE", lambda e: e.memset(C_b[:, 65:HEADS * NC * 66:66], 0.0),
                  deps=[t_vones])
    r_vones.note_w(t_vpad)
    kD_war = r_w1[0].wdep() + r_w1[1].wdep() + r_w2[0].wdep() + r_w2[1].wdep()
    for pt in (P.qbar("SP") + P.qbar("POOL") + P.qbar("ACT") + P.qbar("DVE") +
               [(cs, P.cnt.get(cs, 0)) for cs in ("cPE", "cACT", "cDVE", "cPOOL")]):
        r_wt[0].note_r(pt)
        r_wt[1].note_r(pt)
        r_vfm[0].note_r(pt)
        r_vfm[1].note_r(pt)
    vaug_toks = {}
    proj_last = {}
    for w in (0, 1, 2, 3):
        name = ("q", "k", "v", "kp")[w]
        for m in range(NK):
            wb = m % 4
            t_wt = P.dma(("ACT", "SP")[m % 2], wtproj_v(wb)[:, 0:D], wtbf_d[w, m], deps=r_wt[wb].wdep())
            r_wt[wb].note_w(t_wt)
            for s in range(NS):
                pb = 4 + s % 4 if NS >= 4 else 4 + s % 2
                bank = r_bank[pb]
                deps = bank.wdep() + [t_wt, xnT_cur[s * NST + NST - 1]]
                tokm = None
                for k in range(NK):
                    tokm = P.op("PE", lambda e, k=k, s=s, wb=wb, pb=pb: e.matmul(
                        PSH[pb][:, 0:SW], wtproj_v(wb)[:, k * 128:(k + 1) * 128],
                        xnT_ch(k)[:, s * SW:(s + 1) * SW], start=(k == 0), stop=(k == NK - 1)),
                        deps=(deps if k == 0 else []), sig=(k == NK - 1))
                r_xnT.note_r(tokm)
                r_wt[wb].note_r(tokm)
                bank.note_w(tokm)
                if name == "v":
                    vb = m % 2
                    t_ev = P.op("ACT", lambda e, s=s, vb=vb, pb=pb: e.activation(
                        out=vfm_v(vb)[:, s * SW:(s + 1) * SW], in_=PSH[pb][:, 0:SW], func=AF.Copy),
                        deps=[tokm] + (r_vfm[vb].wdep() if s == 0 else []))
                    bank.note_r(t_ev)
                    if s == NS - 1:
                        r_vfm[vb].note_w(t_ev)
                else:
                    if name == "q":
                        dst = q_ch(m)
                        war = [d for i in regs_of(m * T, (m + 1) * T) for d in r_x[i].wdep()] if s == 0 else []
                    elif name == "kp":
                        dst = kp_ch(m)
                        war = [d for i in regs_of((NK + m) * T, (NK + m + 1) * T) for d in r_x[i].wdep()] if s == 0 else []
                    else:
                        dst = k_ch(m)
                        war = kD_war if (m == 0 and s == 0) else []
                    if s % 2 == 0:
                        t_ev = P.op("ACT", lambda e, dst=dst, s=s, pb=pb: e.activation(
                            out=dst[:, s * SW:(s + 1) * SW], in_=PSH[pb][:, 0:SW], func=AF.Copy),
                            deps=[tokm] + war)
                    else:
                        t_ev = P.op("DVE", lambda e, dst=dst, s=s, pb=pb: e.tensor_copy(
                            out=dst[:, s * SW:(s + 1) * SW], in_=PSH[pb][:, 0:SW]),
                            deps=[tokm] + war)
                    bank.note_r(t_ev)
                    proj_last.setdefault((name, m), []).append(t_ev)
                    if name == "q":
                        for i in regs_of(m * T, (m + 1) * T):
                            r_x[i].note_w(t_ev)
                    elif name == "kp":
                        for i in regs_of((NK + m) * T, (NK + m + 1) * T):
                            r_x[i].note_w(t_ev)
                    else:
                        r_kD.note_w(t_ev)
            if name == "v":
                vb = m % 2
                vtok = r_vfm[vb].rdep()
                tok = None
                for c in range(NC):
                    j = c % 4
                    pbank = r_bank[j]
                    psv = PS_bf[j >> 1][:, (j & 1) * 1024:(j & 1) * 1024 + 128]
                    tokt = P.op("PE", lambda e, c=c, vb=vb, psv=psv: e.matmul(
                        psv, vfm_v(vb)[:, c * 128:(c + 1) * 128], identb_s[:],
                        is_transpose=True, start=True, stop=True),
                        deps=vtok + [t_idb] + pbank.wdep())
                    pbank.note_w(tokt)
                    r_vfm[vb].note_r(tokt)
                    for hh in range(2):
                        h = 2 * m + hh
                        tok = P.op("DVE", lambda e, h=h, c=c, hh=hh, psv=psv: e.tensor_copy(
                            out=vaug(h, c)[:, 0:64], in_=psv[:, hh * 64:(hh + 1) * 64]),
                            deps=[tokt] + (r_vaug.wdep() if (m == 0 and c == 0 and hh == 0) else []) +
                                 ([t_vones] if (m == 0 and c == 0 and hh == 0) else []))
                        vaug_toks[(h, c)] = tok
                        pbank.note_r(tok)
                r_vaug.note_w(tok)

    dump("qkp", A_b[:], [128, 2 * NK * T], BF16, [r_x[i] for i in range(2 * NT)])
    dump("kch", D_b[:, 0:NK * T], [128, NK * T], BF16, [r_kD])
    dump("vaug", C_b[:, 0:HEADS * NC * 66], [128, HEADS * NC * 66], BF16, [r_vaug, r_vones])
    dump("xnT2", B_b[:], [128, NK * T], BF16, [r_xnT])
    for pt in (P.qbar("SP") + P.qbar("POOL") + P.qbar("ACT") + P.qbar("DVE") +
               [(cs, P.cnt.get(cs, 0)) for cs in ("cPE", "cACT", "cDVE", "cPOOL")]):
        for rr in [r_A[0], r_A[1], r_A[2], r_A[3], r_t12, r_scr,
                   r_rsinv[0], r_rsinv[1]] + list(r_oun):
            rr.note_r(pt)
    mix_last = None
    pend = [None]
    rsbi = [0]
    nonlocal_mix = [None]
    pre_w1 = {}
    pre_w2 = {}
    for pair in range(NPAIR):
        hA, hB = 2 * pair, 2 * pair + 1
        q_src = q_ch(pair)
        mix0 = {}
        for br in range(2):
            kv = k_ch(pair) if br == 0 else kp_ch(pair)
            kdeps = proj_last[("k" if br == 0 else "kp", pair)]
            qdeps = proj_last[("q", pair)]
            for s in range(NS):
                depO = r_bank[4].wdep() + r_bank[5].wdep()
                t_oa = t_ob = None
                prevq = []
                t_oa = t_ob = None
                def emit_O(c, abuf):
                    nonlocal t_oa, t_ob
                    dA = [vaug_toks[(hA, c)]] + r_A[abuf].rdep() + ([t_vones] if c == 0 else []) + (depO if c == 0 else [])
                    t_oa = P.op("PE", lambda e, c=c, hA=hA, abuf=abuf: e.matmul(
                        PSH[4][0:65, 0:SW], vaug(hA, c), A_v(abuf)[:, 0:SW],
                        start=(c == 0), stop=(c == NC - 1)),
                        deps=dA, sig=(c == NC - 1))
                    t_ob = P.op("PE", lambda e, c=c, hB=hB, abuf=abuf: e.matmul(
                        PSH[5][0:65, 0:SW], vaug(hB, c), A_v(abuf)[:, 512:512 + SW],
                        start=(c == 0), stop=(c == NC - 1)),
                        deps=[vaug_toks[(hB, c)]], sig=(c == NC - 1))
                    r_A[abuf].note_r(t_ob)
                for c in range(NC):
                    sb = c % 2
                    abuf = c % 4
                    bankA, bankB = r_bank[2 * sb], r_bank[2 * sb + 1]
                    d0 = bankA.wdep() + bankB.wdep() + kdeps + qdeps
                    t_sa = P.op("PE", lambda e, c=c, s=s, sb=sb, kv=kv, q_src=q_src: e.matmul(
                        PS[sb][:, 0:SW], kv[0:64, c * 128:(c + 1) * 128],
                        q_src[0:64, s * SW:(s + 1) * SW], start=True, stop=True), deps=d0,
                        sig=False)
                    t_sb_ = P.op("PE", lambda e, c=c, s=s, sb=sb, kv=kv, q_src=q_src: e.matmul(
                        PS[sb][:, 512:512 + SW], kv[64:128, c * 128:(c + 1) * 128],
                        q_src[64:128, s * SW:(s + 1) * SW], start=True, stop=True), deps=[])
                    bankA.note_w(t_sa)
                    bankB.note_w(t_sb_)
                    ab = r_A[abuf]
                    t_exp = P.op("ACT", lambda e, sb=sb, abuf=abuf: e.activation(
                        out=A_v(abuf), in_=PS[sb][:, 0:1024], func=AF.Exp, scale=0.125),
                        deps=[t_sa, t_sb_] + ab.wdep())
                    bankA.note_r(t_exp)
                    bankB.note_r(t_exp)
                    ab.note_w(t_exp)
                    prevq.append((c, abuf))
                    if len(prevq) > 2:
                        emit_O(*prevq.pop(0))
                    if c == 8 and pend[0] is not None:
                        pend[0]()
                        pend[0] = None
                for item in prevq:
                    emit_O(*item)
                prevq = []
                slot = r_oun[br * NS + s]
                ev_a = P.op("DVE", lambda e, br=br, s=s: e.tensor_copy(
                    out=oun_v(br, s)[0:64, :], in_=PSH[4][0:64, 0:SW]),
                    deps=[t_oa] + slot.wdep())
                ev_b = P.op("DVE", lambda e, br=br, s=s: e.tensor_copy(
                    out=oun_v(br, s)[64:128, :], in_=PSH[5][0:64, 0:SW]),
                    deps=[t_ob])
                r_bank[4].note_r(ev_a)
                r_bank[5].note_r(ev_b)
                slot.note_w(ev_b)
                rb = rsbi[0] % 2
                rsbi[0] += 1
                rsb = rsb_bufs[rb]
                r_rb = r_rsinv[rb]
                rs_a = P.op("ACT", lambda e: e.activation(out=s2row_s[:], in_=PSH[4][64:65, 0:SW],
                                                          func=AF.Copy),
                            deps=[t_oa] + r_s2row.wdep())
                rs_b = P.op("ACT", lambda e: e.activation(out=rdrow_s[:], in_=PSH[5][64:65, 0:SW],
                                                          func=AF.Copy),
                            deps=[t_ob] + r_rdrow.wdep())
                r_bank[4].note_r(rs_a)
                r_bank[5].note_r(rs_b)
                r_s2row.note_w(rs_a)
                r_rdrow.note_w(rs_b)
                rcA = P.op("DVE", lambda e: e.reciprocal(out=s2row_s[:], in_=s2row_s[:]),
                           deps=[rs_a])
                hiA = P.op("DVE", lambda e, rsb=rsb: e.tensor_copy(out=rsb[0], in_=s2row_s[:]),
                           deps=[rcA] + r_rb.wdep())
                loA = P.op("DVE", lambda e, rsb=rsb: e.tensor_tensor(out=rsb[1], in0=s2row_s[:],
                                                                     in1=rsb[0], op=ALU.subtract),
                           deps=[hiA])
                rcB = P.op("DVE", lambda e: e.reciprocal(out=rdrow_s[:], in_=rdrow_s[:]),
                           deps=[rs_b])
                hiB = P.op("DVE", lambda e, rsb=rsb: e.tensor_copy(out=rsb[2], in_=rdrow_s[:]),
                           deps=[rcB])
                loB = P.op("DVE", lambda e, rsb=rsb: e.tensor_tensor(out=rsb[3], in0=rdrow_s[:],
                                                                     in1=rsb[2], op=ALU.subtract),
                           deps=[hiB])
                r_s2row.note_r(loA)
                r_rdrow.note_r(loB)
                r_rb.note_w(loB)

                def norm_mix(pair=pair, br=br, s=s, slot=slot, ev_b=ev_b,
                             loA=loA, loB=loB, rsb=rsb, r_rb=r_rb, mix0=mix0):
                    bbank = r_bank[6 + br]
                    t_bc = None
                    for bi, mc in enumerate((0, 0, 128, 128)):
                        t_bc = P.op("PE", lambda e, br=br, mc=mc, bi=bi, rsb=rsb: e.matmul(
                            PSH[6 + br][:, 0:SW], mk_s[0:1, mc:mc + 128], rsb[bi],
                            start=(bi == 0), stop=(bi == 3)),
                            deps=(([loA, loB, t_mk] + bbank.wdep()) if bi == 0 else []), sig=(bi == 3))
                    bbank.note_w(t_bc)
                    r_rb.note_r(t_bc)
                    if br == 0:
                        m0 = P.op("DVE", lambda e, pair=pair, s=s: e.scalar_tensor_tensor(
                            out=omix_ch(pair)[:, s * SW:(s + 1) * SW], in0=oun_v(0, s)[:, :],
                            scalar=alpha_s[:], in1=PSH[6][:, 0:SW], op0=ALU.mult, op1=ALU.mult),
                            deps=[t_bc, t_al, ev_b] +
                                 (r_xnT.wdep() if (pair == 0 and s == 0) else []))
                        bbank.note_r(m0)
                        slot.note_r(m0)
                        mix0[s] = m0
                    else:
                        m1 = P.op("DVE", lambda e, s=s: e.scalar_tensor_tensor(
                            out=t2_s, in0=oun_v(1, s)[:, :], scalar=oma_s[:], in1=PSH[7][:, 0:SW],
                            op0=ALU.mult, op1=ALU.mult),
                            deps=[t_bc, t_oma, ev_b] + r_t12.wdep())
                        bbank.note_r(m1)
                        slot.note_r(m1)
                        mm = P.op("DVE", lambda e, pair=pair, s=s: e.tensor_tensor(
                            out=omix_ch(pair)[:, s * SW:(s + 1) * SW],
                            in0=omix_ch(pair)[:, s * SW:(s + 1) * SW], in1=t2_s, op=ALU.add),
                            deps=[m1, mix0[s]])
                        r_t12.note_w(mm)
                        nonlocal_mix[0] = mm
                pend[0] = norm_mix
            if br == 0:
                r_kD.note_r(t_sb_)
            for i in regs_of(pair * T, (pair + 1) * T):
                r_x[i].note_r(t_sb_)
            for i in regs_of((NK + pair) * T, (NK + pair + 1) * T):
                r_x[i].note_r(t_sb_)
        for tt2 in range(NT):
            f0, f1 = tt2 * D, (tt2 + 1) * D
            b0, b1 = 2 * f0, 2 * f1
            needed = set()
            for m in range(NK):
                if m * T < b1 and (m + 1) * T > b0:
                    needed.add(m)
                km0, km1 = NK * T + m * T, NK * T + (m + 1) * T
                if km0 < b1 and km1 > b0:
                    needed.add(m)
            if needed and max(needed) == pair:
                tokr = P.dma("POOL", x_t(tt2), xpark_d[tt2 * 128:(tt2 + 1) * 128, :],
                             deps=r_park[tt2].rdep() + r_x[2 * tt2].wdep() + r_x[2 * tt2 + 1].wdep())
                r_x[2 * tt2].note_w(tokr)
                r_x[2 * tt2 + 1].note_w(tokr)
        if pair in (1, 3):
            pe_i = 0 if pair == 1 else 1
            tok_pre = P.dma(nxq(), w1v(pe_i), w1bf_d[pe_i],
                            deps=r_w1[pe_i].wdep() + r_kD.wdep() + [park_toks[pe_i][0]])
            r_w1[pe_i].note_w(tok_pre)
            r_kD.note_w(tok_pre)
            pre_w1[pe_i] = tok_pre
    if pend[0] is not None:
        pend[0]()
        pend[0] = None
    mix_last = nonlocal_mix[0]
    r_xnT.note_w(mix_last)

    dump("omix", B_b[:], [128, NK * T], BF16, [r_xnT])
    t_wo = {}
    wo_bar = P.qbar("SP") + P.qbar("POOL") + P.qbar("DVE")
    for dc in range(NK):
        t_wo[dc] = P.dma("ACT", wo_ch(dc), wotbf_d[dc],
                         deps=(r_kD.wdep() + wo_bar if dc == 0 else []))
        r_kD.note_w(t_wo[dc])
    wo_tile = {}
    for tt in range(NT):
        for cs in range(2):
            yb = (tt * 2 + cs) % 4
            bank = r_bank[yb]
            deps = bank.wdep() + [t_wo[NK - 1], mix_last]
            tokm = None
            for dc in range(NK):
                tokm = P.op("PE", lambda e, dc=dc, tt=tt, cs=cs, yb=yb: e.matmul(
                    PSH[yb][:, 0:512], omix_ch(dc)[:, tt * 128:(tt + 1) * 128],
                    wo_ch(dc)[:, cs * 512:(cs + 1) * 512], start=(dc == 0), stop=(dc == NK - 1)),
                    deps=([t_wo[dc]] + (deps if dc == 0 else [])), sig=(dc == NK - 1))
            r_xnT.note_r(tokm)
            r_kD.note_r(tokm)
            bank.note_w(tokm)
            wo_tile[tt] = tokm
            reg = r_x[2 * tt + cs]
            t_add = P.op("DVE", lambda e, tt=tt, cs=cs, yb=yb: e.tensor_tensor(
                out=x_h(tt, cs), in0=x_h(tt, cs), in1=PSH[yb][:, 0:512], op=ALU.add),
                deps=[tokm] + reg.wdep())
            bank.note_r(t_add)
            reg.note_w(t_add)

    for rr in (r_w1[0], r_w1[1], r_w2[0], r_w2[1]):
        rr.note_r(tokm)
    for _e in (0, 1):
        _t = P.dma(nxq(), w2v(_e), w2bf_d[_e],
                   deps=r_w2[_e].wdep() + [park_toks[_e][1]])
        r_w2[_e].note_w(_t)
        pre_w2[_e] = _t

    dump("x2", A_f[:], [128, NT * D], F32, [r_x[i] for i in range(2 * NT)])

    emit_ln(park=False, tile_war=wo_tile)
    emit_gate()
    ffn2_bar = ([(cs, P.cnt.get(cs, 0)) for cs in ("cPE", "cACT", "cDVE", "cPOOL")] +
                P.qbar("SP") + P.qbar("ACT") + P.qbar("POOL"))
    for _i in range(NSTG):
        for _b in ffn2_bar:
            r_stage[_i].note_r(_b)
    emit_ffn(pre=pre_w1)

    t_out = None
    for tt in range(NT):
        t_out = P.dma(SQ[tt % 3], out_d[tt * 128:(tt + 1) * 128, :], x_t(tt),
                      deps=r_x[2 * tt].rdep() + r_x[2 * tt + 1].rdep())
        r_x[2 * tt].note_r(t_out)
        r_x[2 * tt + 1].note_r(t_out)
    P.op("POOL", lambda e: e.engine_nop(),
         deps=P.qbar("POOL") + P.qbar("SP") + P.qbar("ACT"), sig=False)

    sem_names = ["cPE", "cACT", "cDVE", "cPOOL"]
    for qe in ("SP", "POOL", "ACT", "DVE"):
        for i in range(Plan.NDMA):
            sem_names.append(f"d{qe}{i}")
    import contextlib
    with contextlib.ExitStack() as stack:
        sems = {n: stack.enter_context(nc.semaphore(n)) for n in sem_names}
        with nc.Block() as block:
            @block.tensor
            def _(eng):
                P.emit("PE", eng, sems)
            @block.scalar
            def _(eng):
                P.emit("ACT", eng, sems)
            @block.vector
            def _(eng):
                P.emit("DVE", eng, sems)
            @block.gpsimd
            def _(eng):
                P.emit("POOL", eng, sems)
            @block.sync
            def _(eng):
                P.emit("SP", eng, sems)
    return nc


def kernel(**inputs):
    from concourse.bass_utils import run_bass_kernel_spmd
    T = 2048
    x = np.ascontiguousarray(np.asarray(inputs["L0_attn_x"], dtype=np.float32))
    B = x.shape[0]
    f32 = np.float32
    alpha_bc = np.broadcast_to(np.asarray(inputs["L0_attn_alpha"], dtype=f32).reshape(1, 1), (128, 1)).copy()
    rs_bc = np.broadcast_to(np.asarray(inputs["L0_ffn_route_scale"], dtype=f32).reshape(1, 1), (128, 1)).copy()
    common = {
        "W_Q": np.ascontiguousarray(inputs["L0_attn_W_Q"], dtype=f32),
        "W_K": np.ascontiguousarray(inputs["L0_attn_W_K"], dtype=f32),
        "W_V": np.ascontiguousarray(inputs["L0_attn_W_V"], dtype=f32),
        "W_K_prime": np.ascontiguousarray(inputs["L0_attn_W_K_prime"], dtype=f32),
        "W_O": np.ascontiguousarray(inputs["L0_attn_W_O"], dtype=f32),
        "gate_W": np.ascontiguousarray(inputs["L0_ffn_gate_W"], dtype=f32),
        "W1": np.ascontiguousarray(inputs["L0_ffn_W1"], dtype=f32),
        "b1": np.ascontiguousarray(inputs["L0_ffn_b1"], dtype=f32),
        "W2": np.ascontiguousarray(inputs["L0_ffn_W2"], dtype=f32),
        "b2": np.ascontiguousarray(inputs["L0_ffn_b2"], dtype=f32),
        "Ws1": np.ascontiguousarray(inputs["L0_ffn_Ws1"], dtype=f32),
        "bs1": np.ascontiguousarray(inputs["L0_ffn_bs1"], dtype=f32),
        "Ws2": np.ascontiguousarray(inputs["L0_ffn_Ws2"], dtype=f32),
        "bs2": np.ascontiguousarray(inputs["L0_ffn_bs2"], dtype=f32),
        "alpha_bc": alpha_bc,
        "rs_bc": rs_bc,
        "identf": np.eye(128, dtype=f32),
        "identb": np.eye(128, dtype=ml_dtypes.bfloat16),
        "ones_bf": np.ones((1, 128), dtype=ml_dtypes.bfloat16),
    }
    nc = build_nc(T=T)
    in_maps = [dict(common, x=np.ascontiguousarray(x[b])) for b in range(B)]
    res = run_bass_kernel_spmd(nc, in_maps, core_ids=list(range(B)))
    out = np.stack([res.results[b]["out"] for b in range(B)], axis=0)
    return out
